# version 28
# baseline (speedup 1.0000x reference)
import numpy as np
import concourse.bass as bass
import concourse.mybir as mybir
from concourse.bass_utils import run_bass_kernel_spmd
from contextlib import ExitStack

F32 = mybir.dt.float32
BF16 = mybir.dt.bfloat16
AF = mybir.ActivationFunctionType
ALU = mybir.AluOpType
AX = mybir.AxisListType

T = 2048
NT = 16
NC = 4
TPC = 4
D = 1024
KD = 8
NP = 8
EPS = 1e-5
SCALE = 0.125


class Sem:
    def __init__(self, nc, es, name):
        self.h = es.enter_context(nc.semaphore(name))
        self.n = 0

    def inc(self, k=1):
        self.n += k
        return self.n


def build_nc():
    nc = bass.Bass("TRN2", target_bir_lowering=False)

    x_in = nc.dram_tensor("x", [T, D], F32, kind="ExternalInput").ap()
    wqkT = nc.dram_tensor("wqkT", [D, D], F32, kind="ExternalInput").ap()
    wvT = nc.dram_tensor("wvT", [D, D], F32, kind="ExternalInput").ap()
    woutT = nc.dram_tensor("woutT", [D, D], F32, kind="ExternalInput").ap()
    wffnT = nc.dram_tensor("wffnT", [D, D], F32, kind="ExternalInput").ap()
    wanchorT = nc.dram_tensor("wanchorT", [64, 64], F32, kind="ExternalInput").ap()
    bout = nc.dram_tensor("bout", [1, D], F32, kind="ExternalInput").ap()
    ident_in = nc.dram_tensor("identin", [128, 128], F32, kind="ExternalInput").ap()
    y_out = nc.dram_tensor("y", [T, D], F32, kind="ExternalOutput").ap()
    cc_in = nc.dram_tensor("cc_in", [128, 520], F32, kind="Internal")
    cc_out = nc.dram_tensor("cc_out", [128, 520], F32, kind="Internal")
    cc2_in = nc.dram_tensor("cc2_in", [128, 520], F32, kind="Internal")
    cc2_out = nc.dram_tensor("cc2_out", [128, 520], F32, kind="Internal")
    ccw_in = nc.dram_tensor("ccw_in", [128, 8], F32, kind="Internal")
    ccw_out = nc.dram_tensor("ccw_out", [128, 8], F32, kind="Internal")

    es = ExitStack()
    sb = lambda n, s_, d: es.enter_context(nc.sbuf_tensor(n, s_, d))
    psum = lambda n, s_, d: es.enter_context(nc.psum_tensor(n, s_, d))

    x_sb = sb("x_sb", [128, NT * 1024], F32)
    h_tm = sb("h_tm", [128, 4 * 1024], BF16)
    stats = sb("stats", [128, 8 * 12], F32)
    mv = sb("mv", [128, 8 * 2], F32)
    veps = sb("veps", [128, 8], F32)
    lnv = sb("lnv", [128, 8], F32)
    negm = sb("negm", [128, 8], F32)
    rstd = sb("rstd", [128, 8], F32)
    nmr = sb("nmr", [128, 8], F32)
    h_fm = sb("h_fm", [128, 2 * 4096], BF16)
    wslot_a = sb("wslot_a", [128, 8192], BF16)
    wslot_b = sb("wslot_b", [128, 8192], BF16)
    qk_sb = sb("qk_sb", [128, 2 * 4096], BF16)
    e_am = sb("e_am", [128, 8 * 512], BF16)
    eh_tm = sb("eh_tm", [128, NT * 1024], BF16)
    rs_sb = sb("rs_sb", [128, 64], F32)
    recip_sb = sb("recip_sb", [128, 64], F32)
    v_aug = sb("v_aug", [128, 8 * 1040], BF16)
    agg_sb = sb("agg_sb", [128, 1040], F32)
    agg_sb2 = h_tm[:, 0:2080].bitcast(F32)
    av_blk = sb("av_blk", [128, 1024], BF16)
    anchor_blk = sb("anchor_blk", [128, 128], BF16)
    ident_sb = sb("ident_sb", [128, 128], BF16)
    bias_sb = sb("bias_sb", [1, 1024], BF16)
    ones_row = sb("ones_row", [1, 128], BF16)
    wffn_stage = sb("wffn_stage", [128, 1024], F32)
    final_sb = v_aug
    sguard = sb("sguard", [128, 32], F32)

    pT = psum("pT", [128, 2048], BF16)
    pQ = psum("pQ", [128, 1024], F32)
    pwA = psum("pwA", [128, 1024], F32)
    pwB = psum("pwB", [128, 1024], F32)

    blk = es.enter_context(nc.Block())

    def on(eng, f):
        getattr(blk, eng)(f)

    s = {}
    for name in (["wv%d" % k for k in range(8)] + ["wqk%d" % k for k in range(8)]
                 + ["x0", "x0b", "x1", "x2", "x3", "wsm", "wqk", "wv", "wout",
                 "wf0", "wf1", "st0", "st1", "init", "ve", "h", "ht",
                 "hfe", "qkp", "qke", "vp", "vev", "scp", "ee", "etb",
                 "norm", "agp", "age", "sgn", "ccin", "cc",
                 "ccout", "av", "eat", "eae", "bcp", "bce", "wop", "r1", "rsd", "rsd2", "algn",
                 "ve2", "h2", "h2t", "h2fe", "ffp", "r2"]):
        s[name] = Sem(nc, es, "s_" + name)

    tok = {}
    for c in range(NC):
        tok[("x", c)] = 16
    tok["init"] = 1
    tok["wsm"] = 16 * 4
    tok["wqk"] = 16 * KD
    tok["wv"] = 16 * KD
    tok["wout"] = 16 * KD
    for t in range(NT):
        for nm_ in ["h", "ht", "hfe", "etb", "norm",
                    "wop", "r1", "ve2", "h2", "h2t", "h2fe",
                    "ffp", "r2"]:
            tok[(nm_, t)] = t + 1
        tok[("st", t)] = 16 * (t // 2 + 1)

    for m in range(NC):
        for p in range(NP):
            for tw in range(TPC):
                tok[("agp", 4 * m + tw, p)] = 32 * m + 4 * p + tw + 1
            tok[("age", m, p)] = 8 * m + p + 1
    for c in range(NC):
        tok[("ve", c)] = c + 1
        tok[("rsd", c)] = c + 1
    for pi in range(NT // 2):
        tok[("ve2p", pi)] = pi + 1
        tok[("rsd2", pi)] = pi + 1
    for c in range(NC):
        for i in range(8):
            for nm_ in ["qkp", "qke", "vp", "vev", "scp", "ee",
                        "eat", "eae", "bcp", "bce"]:
                tok[(nm_, c, i)] = c * 8 + i + 1
        tok[("qklast", c)] = tok[("qkp", c, KD - 1)]
        tok[("vlast", c)] = tok[("vp", c, 7)]
        tok[("vevlast", c)] = tok[("vev", c, 7)]
    for j in range(KD):
        tok[("sgn", j)] = j + 1
        tok[("wf", j)] = 16 * (j + 1)
    tok["ccin"] = 48
    tok["cc"] = 1
    tok["ccout"] = 32
    for p in range(NP):
        tok[("av", p)] = p + 1

    def rec(key, val):
        assert tok.get(key, val) == val, (key, tok.get(key), val)
        tok[key] = val

    def init_sync(eng):
        eng.dma_start(
            out=x_sb[:, 0:2048].rearrange("p (a d) -> p a d", a=2),
            in_=x_in[0:256, :].rearrange("(a p) d -> p a d", p=128),
        ).then_inc(s["x0"].h, 16)
        rec(("x", 0), s["x0"].inc(16))
        eng.dma_start(
            out=x_sb[:, 2048:4096].rearrange("p (a d) -> p a d", a=2),
            in_=x_in[256:512, :].rearrange("(a p) d -> p a d", p=128),
        ).then_inc(s["x0b"].h, 16)
        rec(("x0b", 0), s["x0b"].inc(16))
        for c in range(1, NC):
            eng.dma_start(
                out=x_sb[:, 4096 * c:4096 * (c + 1)].rearrange("p (a d) -> p a d", a=4),
                in_=x_in[512 * c:512 * (c + 1), :].rearrange("(a p) d -> p a d", p=128),
            ).then_inc(s["x%d" % c].h, 16)
            rec(("x", c), s["x%d" % c].inc(16))
    on("sync", init_sync)

    def init_gp(eng):
        eng.memset(ones_row[:], 1.0)
        eng.memset(v_aug[:], 1.0)
        eng.memset(av_blk[:], 0.0)
        eng.memset(anchor_blk[:], 0.0)
        eng.drain()
        eng.nop().then_inc(s["init"].h, 1)
        rec("init", s["init"].inc(1))
        eng.dma_start(out=anchor_blk[0:64, 0:64], in_=wanchorT).then_inc(s["wsm"].h, 16)
        s["wsm"].inc(16)
        eng.dma_start(out=anchor_blk[64:128, 64:128], in_=wanchorT).then_inc(s["wsm"].h, 16)
        s["wsm"].inc(16)
        eng.dma_start(out=ident_sb[:], in_=ident_in).then_inc(s["wsm"].h, 16)
        s["wsm"].inc(16)
        eng.dma_start(out=bias_sb[:], in_=bout).then_inc(s["wsm"].h, 16)
        rec("wsm", s["wsm"].inc(16))
        for k in range(KD):
            eng.dma_start(out=wslot_b[:, 1024 * k:1024 * (k + 1)],
                          in_=wvT[128 * k:128 * (k + 1), :]).then_inc(s["wv%d" % k].h, 16)
            s["wv%d" % k].inc(16)
        for k in range(KD):
            eng.dma_start(out=wslot_a[:, 1024 * k:1024 * (k + 1)],
                          in_=wqkT[128 * k:128 * (k + 1), :]).then_inc(s["wqk%d" % k].h, 16)
            s["wqk%d" % k].inc(16)
        eng.memset(sguard[:, 24:32], 0.0)
        eng.drain()
        eng.dma_start(out=ccw_in.ap(), in_=sguard[:, 24:32]).then_inc(s["ccin"].h, 16)
        s["ccin"].inc(16)
        eng.wait_ge(s["ccin"].h, 16)
        eng.collective_compute("AllReduce", ALU.add,
                               [[0, 1], [2, 3], [4, 5], [6, 7]],
                               ins=[ccw_in.ap()], outs=[ccw_out.ap()]
                               ).then_inc(s["init"].h, 1)
        s["init"].inc(1)
        eng.collective_compute("AllReduce", ALU.add,
                               [[0, 1], [2, 3], [4, 5], [6, 7]],
                               ins=[ccw_in.ap()], outs=[ccw_out.ap()]
                               ).then_inc(s["algn"].h, 1)
        s["algn"].inc(1)
    on("gpsimd", init_gp)


    def emit_A_statsA(c, scalar_too=True):
        so = 4 * (c % 2)

        def ln_stats(v, c=c, so=so):
            if c == 3:
                v.wait_ge(s["algn"].h, 1)
            v.wait_ge(s["x%d" % c].h, tok[("x", c)])
            if c >= 2:
                v.wait_ge(s["h"].h, tok[("h", 4 * (c - 2) + 3)])
            for tw in range(TPC):
                if c == 0 and tw == 2:
                    v.wait_ge(s["x0b"].h, tok[("x0b", 0)])
                t = 4 * c + tw
                xs = x_sb[:, 1024 * t:1024 * (t + 1)]
                v.bn_stats(stats[:, 12 * (so + tw):12 * (so + tw) + 6], xs[:, 0:512])
                v.bn_stats(stats[:, 12 * (so + tw) + 6:12 * (so + tw) + 12],
                           xs[:, 512:1024])
            v.drain()
            for tw in range(TPC):
                v.bn_aggr(mv[:, 2 * (so + tw):2 * (so + tw) + 2],
                          stats[:, 12 * (so + tw):12 * (so + tw) + 12])
            v.drain()
            mvv = mv[:, 2 * so:2 * so + 8].rearrange("p (a b) -> p a b", b=2)
            v.tensor_scalar(veps[:, so:so + 4], mvv[:, :, 1:2].squeeze(2),
                            EPS, None, ALU.add)
            v.tensor_scalar(negm[:, so:so + 4], mvv[:, :, 0:1].squeeze(2),
                            -1.0, None, ALU.mult).then_inc(s["ve"].h, 1)
            rec(("ve", c), s["ve"].inc(1))
        on("vector", ln_stats)

        def ln_scalar(sc, c=c, so=so):
            if c == 3:
                sc.wait_ge(s["algn"].h, 1)
            sc.wait_ge(s["ve"].h, tok[("ve", c)])
            sc.activation(lnv[:, so:so + 4], veps[:, so:so + 4], AF.Ln)
            sc.drain()
            sc.activation(rstd[:, so:so + 4], lnv[:, so:so + 4], AF.Exp,
                          scale=-0.5).then_inc(s["rsd"].h, 1)
            rec(("rsd", c), s["rsd"].inc(1))
        if scalar_too:
            on("scalar", ln_scalar)
        else:
            emit_A_statsA.deferred = ln_scalar

    def emit_A_statsB(c):
        so = 4 * (c % 2)

        def ln_apply(v, c=c, so=so):
            v.wait_ge(s["rsd"].h, tok[("rsd", c)])
            v.scalar_tensor_tensor(nmr[:, so:so + 4], negm[:, so:so + 4], 1.0,
                                   rstd[:, so:so + 4], ALU.mult, ALU.mult)
            v.drain()
            for tw in range(TPC):
                t = 4 * c + tw
                if c >= 1 and tw == 0:
                    v.wait_ge(s["ht"].h, tok[("ht", 4 * (c - 1) + 3)])
                v.tensor_scalar(h_tm[:, 1024 * tw:1024 * (tw + 1)],
                                x_sb[:, 1024 * t:1024 * (t + 1)],
                                rstd[:, so + tw:so + tw + 1],
                                nmr[:, so + tw:so + tw + 1],
                                ALU.mult, ALU.add).then_inc(s["h"].h, 1)
                rec(("h", t), s["h"].inc(1))
        on("vector", ln_apply)

    def emit_A1(c):
        for tw in range(TPC):
            t = 4 * c + tw
            sl = t % 2

            def h_transpose(te, t=t, sl=sl, c=c, tw=tw):
                if c == 3 and tw == 0:
                    te.wait_ge(s["algn"].h, 1)
                te.wait_ge(s["h"].h, tok[("h", t)])
                te.wait_ge(s["wsm"].h, tok["wsm"])
                if t >= 2:
                    te.wait_ge(s["hfe"].h, tok[("hfe", t - 2)])
                dst = pT[:, 1024 * sl:1024 * (sl + 1)]
                for k in range(KD):
                    ins = te.matmul(dst[:, 128 * k:128 * (k + 1)],
                                    h_tm[:, 1024 * tw + 128 * k:1024 * tw + 128 * (k + 1)],
                                    ident_sb[:], start=(k == 0), stop=(k == KD - 1),
                                    is_transpose=True)
                ins.then_inc(s["ht"].h, 1)
                rec(("ht", t), s["ht"].inc(1))
            on("tensor", h_transpose)

            def hfm_evac(v, t=t, tw=tw, c=c, sl=sl):
                v.wait_ge(s["ht"].h, tok[("ht", t)])
                if c >= 2:
                    v.wait_ge(s["vp"].h, tok[("vlast", c - 2)])
                    v.wait_ge(s["qkp"].h, tok[("qklast", c - 2)])
                src = pT[:, 1024 * sl:1024 * (sl + 1)]
                dst = (h_fm[:, 4096 * (c % 2):4096 * (c % 2 + 1)]
                       .rearrange("p (k q) -> p k q", k=KD)[:, :, 128 * tw:128 * (tw + 1)])
                v.tensor_copy(dst, src.rearrange("p (k q) -> p k q", k=KD)
                              ).then_inc(s["hfe"].h, 1)
                rec(("hfe", t), s["hfe"].inc(1))
            on("vector", hfm_evac)

    def emit_A2(c):
        def emb_v(te, c=c):
            for tw in range(TPC):
                te.wait_ge(s["hfe"].h, tok[("hfe", 4 * c + tw)])
                for vo in range(2):
                    g = 2 * tw + vo
                    half = g % 2
                    seq = c * 8 + g
                    if seq >= 2:
                        gg = seq - 2
                        te.wait_ge(s["vev"].h, tok[("vev", gg // 8, gg % 8)])
                    if c >= 2 and g < 2:
                        te.wait_ge(s["norm"].h, tok[("norm", 4 * (c - 2) + g)])
                    for k in range(KD):
                        if c == 0 and tw == 0:
                            te.wait_ge(s["wv%d" % k].h, 16)
                        ins = te.matmul(pwA[:, 512 * half:512 * (half + 1)],
                                        h_fm[:, 4096 * (c % 2) + 512 * k + 128 * tw:
                                             4096 * (c % 2) + 512 * k + 128 * (tw + 1)],
                                        wslot_b[:, 1024 * k + 512 * vo:
                                                1024 * k + 512 * (vo + 1)],
                                        start=(k == 0), stop=(k == KD - 1))
                    ins.then_inc(s["vp"].h, 1)
                    rec(("vp", c, g), s["vp"].inc(1))
        on("tensor", emb_v)

        def v_evac(v, c=c):
            for tw in range(TPC):
                t = 4 * c + tw
                tsl = t % 8
                for vo in range(2):
                    g = 2 * tw + vo
                    half = g % 2
                    v.wait_ge(s["vp"].h, tok[("vp", c, g)])
                    if t >= 8:
                        v.wait_ge(s["agp"].h, tok[("agp", t - 8, NP - 1)])
                    if t == 0 and vo == 0:
                        v.wait_ge(s["init"].h, tok["init"])
                    src = pwA[:, 512 * half:512 * (half + 1)].rearrange(
                        "p (q h d) -> p q h d", q=4, h=2)
                    base = v_aug[:, 1040 * tsl + 520 * vo:1040 * tsl + 520 * (vo + 1)]
                    dst = (base.rearrange("p (q r) -> p q r", q=4)
                           .rearrange("p q (h e) -> p q h e", h=2)[:, :, :, 0:64])
                    v.tensor_copy(dst, src).then_inc(s["vev"].h, 1)
                    rec(("vev", c, g), s["vev"].inc(1))
        on("vector", v_evac)

        def emb_qk(te, c=c):
            for o in range(KD):
                half = o % 2
                seq = c * KD + o
                if seq >= 2:
                    oo = seq - 2
                    te.wait_ge(s["qke"].h, tok[("qke", oo // KD, oo % KD)])
                if c >= 2 and o < 2:
                    te.wait_ge(s["age"].h, tok[("age", c - 2, 6 + o)])
                for k in range(KD):
                    if c == 0 and o == 0:
                        te.wait_ge(s["wqk%d" % k].h, 16)
                    ins = te.matmul(pQ[:, 512 * half:512 * (half + 1)],
                                    wslot_a[:, 1024 * k + 128 * o:1024 * k + 128 * (o + 1)],
                                    h_fm[:, 4096 * (c % 2) + 512 * k:
                                         4096 * (c % 2) + 512 * (k + 1)],
                                    start=(k == 0), stop=(k == KD - 1))
                ins.then_inc(s["qkp"].h, 1)
                rec(("qkp", c, o), s["qkp"].inc(1))
        on("tensor", emb_qk)

        def qk_evac(v, c=c):
            for o in range(KD):
                half = o % 2
                v.wait_ge(s["qkp"].h, tok[("qkp", c, o)])
                if c >= 2:
                    v.wait_ge(s["scp"].h, tok[("scp", c - 2, o)])
                v.tensor_copy(qk_sb[:, 4096 * (c % 2) + 512 * o:
                                    4096 * (c % 2) + 512 * (o + 1)],
                              pQ[:, 512 * half:512 * (half + 1)]).then_inc(s["qke"].h, 1)
                rec(("qke", c, o), s["qke"].inc(1))
        on("vector", qk_evac)

    def emit_M(m):
        et_banks = [pwA[:, 0:512].bitcast(BF16), pwA[:, 512:1024].bitcast(BF16),
                    pwB[:, 0:512].bitcast(BF16), pwB[:, 512:1024].bitcast(BF16)]

        def te_middle(te, m=m):
            for p in range(NP):
                half = p % 2
                if p < 2:
                    if m < NC - 1:
                        te.wait_ge(s["qke"].h, tok[("qke", m + 1, 6 + p)])
                    else:
                        te.wait_ge(s["age"].h, tok[("age", m - 1, 6 + p)])
                else:
                    te.wait_ge(s["ee"].h, tok[("ee", m, p - 2)])
                te.wait_ge(s["qke"].h, tok[("qke", m, p)])
                te.matmul(pQ[:, 512 * half:512 * (half + 1)], anchor_blk[:],
                          qk_sb[:, 4096 * (m % 2) + 512 * p:4096 * (m % 2) + 512 * (p + 1)],
                          start=True, stop=True).then_inc(s["scp"].h, 1)
                rec(("scp", m, p), s["scp"].inc(1))
            for tw in range(TPC):
                t = 4 * m + tw
                if tw < 2:
                    if m < NC - 1:
                        te.wait_ge(s["vev"].h, tok[("vev", m + 1, 6 + tw)])
                    else:
                        te.wait_ge(s["norm"].h, tok[("norm", 4 * (m - 1) + tw)])
                else:
                    if m >= 1:
                        te.wait_ge(s["norm"].h, tok[("norm", 4 * (m - 1) + tw)])
                for p in range(NP):
                    te.wait_ge(s["ee"].h, tok[("ee", m, p)])
                    ins = te.matmul(et_banks[tw][:, 128 * p:128 * (p + 1)],
                                    e_am[:, 512 * p + 128 * tw:512 * p + 128 * (tw + 1)],
                                    ident_sb[:], start=(p == 0), stop=(p == NP - 1),
                                    is_transpose=True)
                ins.then_inc(s["etb"].h, 1)
                rec(("etb", t), s["etb"].inc(1))
            for p in range(NP):
                half = p % 2
                if p < 2:
                    te.wait_ge(s["ee"].h, tok[("ee", m, 6 + p)])
                else:
                    te.wait_ge(s["age"].h, tok[("age", m, p - 2)])
                for tw in range(TPC):
                    t = 4 * m + tw
                    tsl = t % 8
                    if p == 0:
                        te.wait_ge(s["norm"].h, tok[("norm", t)])
                    te.matmul(pQ[:, 512 * half:512 * half + 130],
                              eh_tm[:, 1024 * t + 128 * p:1024 * t + 128 * (p + 1)],
                              v_aug[:, 1040 * tsl + 130 * p:1040 * tsl + 130 * (p + 1)],
                              start=(tw == 0), stop=(tw == TPC - 1)
                              ).then_inc(s["agp"].h, 1)
                    rec(("agp", t, p), s["agp"].inc(1))
        on("tensor", te_middle)

        def exp_evac(sc, m=m):
            if m == 2:
                sc.wait_ge(s["cc"].h, 1)
            for p in range(NP):
                half = p % 2
                sc.wait_ge(s["scp"].h, tok[("scp", m, p)])
                if m >= 1:
                    sc.wait_ge(s["etb"].h, tok[("etb", 4 * (m - 1) + 3)])
                sc.activation(e_am[:, 512 * p:512 * (p + 1)],
                              pQ[:, 512 * half:512 * (half + 1)],
                              AF.Exp, scale=SCALE).then_inc(s["ee"].h, 1)
                rec(("ee", m, p), s["ee"].inc(1))
        on("scalar", exp_evac)

        def dve_middle(v, m=m):
            if m == 2:
                v.wait_ge(s["cc"].h, 1)
            def emit_norm_pair(tw0):
                for tw in (tw0, tw0 + 1):
                    t = 4 * m + tw
                    v.wait_ge(s["etb"].h, tok[("etb", t)])
                    src4 = et_banks[tw].rearrange("p (g h a) -> p g h a", g=NP, h=2)
                    v.tensor_reduce(rs_sb[:, 16 * tw:16 * (tw + 1)]
                                    .rearrange("p (g h) -> p g h", g=NP).unsqueeze(3),
                                    src4, AX.X, ALU.add)
                v.drain()
                v.reciprocal(recip_sb[:, 16 * tw0:16 * (tw0 + 2)],
                             rs_sb[:, 16 * tw0:16 * (tw0 + 2)])
                v.drain()
                for tw in (tw0, tw0 + 1):
                    t = 4 * m + tw
                    src4 = et_banks[tw].rearrange("p (g h a) -> p g h a", g=NP, h=2)
                    rb = (recip_sb[:, 16 * tw:16 * (tw + 1)]
                          .rearrange("p (g h) -> p g h", g=NP).unsqueeze(3)
                          .broadcast_to([128, NP, 2, 64]))
                    v.tensor_tensor(eh_tm[:, 1024 * t:1024 * (t + 1)]
                                    .rearrange("p (g h a) -> p g h a", g=NP, h=2),
                                    src4, rb, ALU.mult).then_inc(s["norm"].h, 1)
                    rec(("norm", t), s["norm"].inc(1))

            emit_norm_pair(0)
            emit_norm_pair(2)
            for p in range(NP):
                half = p % 2
                v.wait_ge(s["agp"].h, tok[("agp", 4 * m + 3, p)])
                src = pQ[:, 512 * half:512 * half + 130]
                buf = agg_sb if m < 2 else agg_sb2
                dst = buf[:, 130 * p:130 * (p + 1)]
                if m in (0, 2):
                    ins = v.tensor_copy(dst, src)
                else:
                    ins = v.tensor_tensor(dst, dst, src, ALU.add)
                ins.then_inc(s["age"].h, 1)
                rec(("age", m, p), s["age"].inc(1))
        on("vector", dve_middle)

    emit_A_statsA(0)
    emit_A_statsB(0)
    emit_A1(0)
    emit_A_statsA(1)
    emit_A2(0)
    emit_A_statsB(1)
    emit_A1(1)
    emit_A_statsA(2)
    emit_A2(1)
    emit_A_statsB(2)
    emit_A1(2)
    emit_A_statsA(3, scalar_too=False)
    emit_M(0)
    on("scalar", emit_A_statsA.deferred)
    emit_A2(2)
    emit_A_statsB(3)
    emit_A1(3)
    emit_M(1)
    emit_A2(3)
    emit_M(2)
    emit_M(3)

    def late_weights(eng):
        eng.wait_ge(s["qkp"].h, tok[("qklast", NC - 1)])
        for k in range(KD):
            eng.dma_start(out=wslot_a[:, 1024 * k:1024 * (k + 1)],
                          in_=woutT[128 * k:128 * (k + 1), :]).then_inc(s["wout"].h, 16)
            s["wout"].inc(16)
    on("gpsimd", late_weights)

    def wffn_loads(eng):
        for j in range(KD):
            if j >= 1:
                eng.wait_ge(s["sgn"].h, tok[("sgn", j - 1)])
            eng.dma_start(out=wffn_stage[:],
                          in_=wffnT[128 * j:128 * (j + 1), :]).then_inc(s["wf0"].h, 16)
            rec(("wf", j), s["wf0"].inc(16))
    on("sync", wffn_loads)

    def sign_ffn(sc):
        sc.wait_ge(s["vp"].h, tok[("vlast", NC - 1)])
        for j in range(KD):
            sc.wait_ge(s["wf0"].h, tok[("wf", j)])
            sc.sign(wslot_b[:, 1024 * j:1024 * (j + 1)],
                    wffn_stage[:]).then_inc(s["sgn"].h, 1)
            rec(("sgn", j), s["sgn"].inc(1))
    on("scalar", sign_ffn)

    def collective(eng):
        def compact(dst_dram, sb_src):
            top = sb_src[0:64, :].rearrange("p (g r) -> p g r", g=NP)[:, :, 0:65]
            bot = sb_src[64:128, :].rearrange("p (g r) -> p g r", g=NP)[:, :, 65:130]
            eng.dma_start(out=dst_dram.ap()[0:64, :].rearrange("p (g r) -> p g r", g=NP),
                          in_=top).then_inc(s["ccin"].h, 16)
            s["ccin"].inc(16)
            eng.dma_start(out=dst_dram.ap()[64:128, :].rearrange("p (g r) -> p g r", g=NP),
                          in_=bot).then_inc(s["ccin"].h, 16)
            s["ccin"].inc(16)

        eng.wait_ge(s["age"].h, tok[("age", 1, NP - 1)])
        compact(cc_in, agg_sb[:])
        eng.wait_ge(s["ccin"].h, s["ccin"].n)
        eng.collective_compute("AllReduce", ALU.add,
                               [[0, 1], [2, 3], [4, 5], [6, 7]],
                               ins=[cc_in.ap()], outs=[cc_out.ap()]
                               ).then_inc(s["cc"].h, 1)
        s["cc"].inc(1)
        eng.wait_ge(s["age"].h, tok[("age", NC - 1, NP - 1)])
        compact(cc2_in, agg_sb2[:, 0:1040])
        eng.wait_ge(s["ccin"].h, s["ccin"].n)
        eng.collective_compute("AllReduce", ALU.add,
                               [[0, 1], [2, 3], [4, 5], [6, 7]],
                               ins=[cc2_in.ap()], outs=[cc2_out.ap()]
                               ).then_inc(s["cc"].h, 1)
        s["cc"].inc(1)
        eng.wait_ge(s["cc"].h, s["cc"].n)
        eng.dma_start(out=agg_sb[:, 0:520], in_=cc_out.ap()).then_inc(s["ccout"].h, 16)
        s["ccout"].inc(16)
        eng.dma_start(out=agg_sb[:, 520:1040], in_=cc2_out.ap()).then_inc(s["ccout"].h, 16)
        rec("ccout", s["ccout"].inc(16))
    on("gpsimd", collective)

    def av_prep(v):
        v.wait_ge(s["ccout"].h, tok["ccout"])
        v.tensor_tensor(agg_sb[:, 0:520], agg_sb[:, 0:520], agg_sb[:, 520:1040],
                        ALU.add)
        v.drain()
        scol = agg_sb[:, 0:520].rearrange("p (g r) -> p g r", g=NP)[:, :, 64:65].squeeze(2)
        iszero = sguard[:, 0:8]
        ssafe = sguard[:, 8:16]
        rg = sguard[:, 24:32]
        v.tensor_single_scalar(iszero, scol, 0.0, ALU.is_equal)
        v.scalar_tensor_tensor(ssafe, scol, 0.0, scol, ALU.is_equal, ALU.add)
        v.drain()
        v.reciprocal(rg, ssafe)
        v.drain()
        v.tensor_tensor(rg, rg, iszero, ALU.subtract)
        v.drain()
        for p in range(NP):
            v.tensor_scalar(av_blk[0:64, 128 * p:128 * p + 64],
                            agg_sb[0:64, 65 * p:65 * p + 64],
                            rg[0:64, p:p + 1], None, ALU.mult)
            v.tensor_scalar(av_blk[64:128, 128 * p + 64:128 * p + 128],
                            agg_sb[64:128, 65 * p:65 * p + 64],
                            rg[64:128, p:p + 1], None,
                            ALU.mult).then_inc(s["av"].h, 1)
            rec(("av", p), s["av"].inc(1))
    c0_halves_bf = [pT[:, 0:1024], pT[:, 1024:2048]]
    c0_halves = [None, None, pwA[:, 0:512], pwA[:, 512:1024],
                 pwB[:, 0:512], pwB[:, 512:1024], pQ[:, 0:512], pQ[:, 512:1024]]

    def _st2_c0_evac_a(v):
        for p in range(NP):
            v.wait_ge(s["eat"].h, tok[("eat", 0, p)])
            srcp = (c0_halves_bf[p] if p < 2
                    else c0_halves[p].bitcast(BF16))[:, 0:512]
            v.tensor_copy(qk_sb[:, 512 * p:512 * (p + 1)],
                          srcp).then_inc(s["eae"].h, 1)
            rec(("eae", 0, p), s["eae"].inc(1))
    on("vector", _st2_c0_evac_a)
    on("vector", av_prep)

    def _st2_c0(te):
        for p in range(NP):
            if p < 2:
                te.wait_ge(s["hfe"].h, tok[("hfe", NT - 1)])
            elif p < 6:
                te.wait_ge(s["norm"].h, tok[("norm", 12 + (p - 2))])
            else:
                te.wait_ge(s["age"].h, tok[("age", NC - 1, p)])
            dst = c0_halves_bf[p] if p < 2 else c0_halves[p].bitcast(BF16)
            for tw in range(TPC):
                ins = te.matmul(dst[:, 128 * tw:128 * (tw + 1)],
                                eh_tm[:, 1024 * tw + 128 * p:1024 * tw + 128 * (p + 1)],
                                ident_sb[:], start=(tw == 0), stop=(tw == TPC - 1),
                                is_transpose=True)
            ins.then_inc(s["eat"].h, 1)
            rec(("eat", 0, p), s["eat"].inc(1))
        for p in range(NP):
            half = p % 2
            te.wait_ge(s["av"].h, tok[("av", p)])
            if p < 2:
                te.wait_ge(s["eae"].h, tok[("eae", 0, 6 + p)])
            else:
                te.wait_ge(s["bce"].h, tok[("bce", 0, p - 2)])
            te.wait_ge(s["eae"].h, tok[("eae", 0, p)])
            te.matmul(pQ[:, 512 * half:512 * (half + 1)],
                      av_blk[:, 128 * p:128 * (p + 1)],
                      qk_sb[:, 512 * p:512 * (p + 1)],
                      start=True, stop=True).then_inc(s["bcp"].h, 1)
            rec(("bcp", 0, p), s["bcp"].inc(1))
    def _st2_c0_evac(v):
        for p in range(NP):
            half = p % 2
            v.wait_ge(s["bcp"].h, tok[("bcp", 0, p)])
            v.tensor_copy(final_sb[:, 512 * p:512 * (p + 1)],
                          pQ[:, 512 * half:512 * (half + 1)]).then_inc(s["bce"].h, 1)
            rec(("bce", 0, p), s["bce"].inc(1))
    def _st2_pq(te, c):
        for p in range(NP):
            half = p % 2
            if p < 2:
                te.wait_ge(s["bce"].h, tok[("bce", c - 1, 6 + p)])
            else:
                te.wait_ge(s["bce"].h, tok[("bce", c, p - 2)])
            dst = pQ[:, 512 * half:512 * (half + 1)].bitcast(BF16)
            for tw in range(TPC):
                t = 4 * c + tw
                ins = te.matmul(dst[:, 128 * tw:128 * (tw + 1)],
                                eh_tm[:, 1024 * t + 128 * p:1024 * t + 128 * (p + 1)],
                                ident_sb[:], start=(tw == 0), stop=(tw == TPC - 1),
                                is_transpose=True)
            ins.then_inc(s["eat"].h, 1)
            rec(("eat", c, p), s["eat"].inc(1))
            te.wait_ge(s["eae"].h, tok[("eae", c, p)])
            te.matmul(pQ[:, 512 * half:512 * (half + 1)],
                      av_blk[:, 128 * p:128 * (p + 1)],
                      qk_sb[:, 4096 * (c % 2) + 512 * p:
                           4096 * (c % 2) + 512 * (p + 1)],
                      start=True, stop=True).then_inc(s["bcp"].h, 1)
            rec(("bcp", c, p), s["bcp"].inc(1))

    def _st2_pq_evac(sc, c, plist):
        for p in plist:
            half = p % 2
            sc.wait_ge(s["eat"].h, tok[("eat", c, p)])
            if c >= 2:
                sc.wait_ge(s["bcp"].h, tok[("bcp", c - 2, p)])
            sc.copy(qk_sb[:, 4096 * (c % 2) + 512 * p:
                          4096 * (c % 2) + 512 * (p + 1)],
                    pQ[:, 512 * half:512 * (half + 1)].bitcast(BF16)[:, 0:512]
                    ).then_inc(s["eae"].h, 1)
            rec(("eae", c, p), s["eae"].inc(1))
            sc.wait_ge(s["bcp"].h, tok[("bcp", c, p)])
            if p == 0 and c >= 2:
                sc.wait_ge(s["wop"].h, tok[("wop", 4 * (c - 2) + 3)])
            sc.copy(final_sb[:, 4096 * (c % 2) + 512 * p:
                             4096 * (c % 2) + 512 * (p + 1)],
                    pQ[:, 512 * half:512 * (half + 1)]).then_inc(s["bce"].h, 1)
            rec(("bce", c, p), s["bce"].inc(1))

    on("tensor", _st2_c0)
    on("vector", _st2_c0_evac)

    for c in range(NC):
        def wout_mm(te, c=c):
            for tw in range(TPC):
                t = 4 * c + tw
                bank = pwA if tw % 2 == 0 else pwB
                if tw >= 2:
                    te.wait_ge(s["r1"].h, tok[("r1", t - 2)])
                elif c >= 1:
                    te.wait_ge(s["r2"].h, tok[("r2", t - 2)])
                elif tw == 0:
                    te.wait_ge(s["eae"].h, tok[("eae", 0, 5)])
                    te.wait_ge(s["init"].h, tok["init"])
                    te.wait_ge(s["wsm"].h, tok["wsm"])
                    te.wait_ge(s["wout"].h, tok["wout"])
                for j in range(2):
                    te.matmul(bank[:, 512 * j:512 * (j + 1)], ones_row[:],
                              bias_sb[:, 512 * j:512 * (j + 1)], start=True, stop=False)
                for k in range(KD):
                    if tw == 0:
                        te.wait_ge(s["bce"].h, tok[("bce", c, k)])
                    for j in range(2):
                        ins = te.matmul(bank[:, 512 * j:512 * (j + 1)],
                                        final_sb[:, 4096 * (c % 2) + 512 * k + 128 * tw:
                                                4096 * (c % 2) + 512 * k + 128 * (tw + 1)],
                                        wslot_a[:, 1024 * k + 512 * j:
                                                1024 * k + 512 * (j + 1)],
                                        start=False, stop=(k == KD - 1))
                ins.then_inc(s["wop"].h, 1)
                rec(("wop", t), s["wop"].inc(1))
        on("tensor", wout_mm)
        if c + 1 < NC:
            on("tensor", lambda te, c=c: _st2_pq(te, c + 1))

        for j2 in range(2):
            t0 = 4 * c + 2 * j2
            pi = t0 // 2
            so2 = 2 * (pi % 2)

            def resid1_pair(v, t0=t0, pi=pi, so2=so2, j2=j2):
                for i in range(2):
                    t = t0 + i
                    bank = pwA if t % 2 == 0 else pwB
                    v.wait_ge(s["wop"].h, tok[("wop", t)])
                    xs = x_sb[:, 1024 * t:1024 * (t + 1)]
                    v.tensor_tensor(xs, xs, bank[:], ALU.add).then_inc(s["r1"].h, 1)
                    rec(("r1", t), s["r1"].inc(1))
                v.drain()
                if pi >= 2:
                    v.wait_ge(s["h2"].h, tok[("h2", t0 - 3)])
                for i in range(2):
                    t = t0 + i
                    xs = x_sb[:, 1024 * t:1024 * (t + 1)]
                    v.bn_stats(stats[:, 12 * (so2 + i):12 * (so2 + i) + 6],
                               xs[:, 0:512])
                    v.bn_stats(stats[:, 12 * (so2 + i) + 6:12 * (so2 + i) + 12],
                               xs[:, 512:1024])
                v.drain()
                for i in range(2):
                    v.bn_aggr(mv[:, 2 * (so2 + i):2 * (so2 + i) + 2],
                              stats[:, 12 * (so2 + i):12 * (so2 + i) + 12])
                v.drain()
                mvv = mv[:, 2 * so2:2 * so2 + 4].rearrange("p (a b) -> p a b", b=2)
                v.tensor_scalar(veps[:, so2:so2 + 2], mvv[:, :, 1:2].squeeze(2),
                                EPS, None, ALU.add)
                v.tensor_scalar(negm[:, so2:so2 + 2], mvv[:, :, 0:1].squeeze(2),
                                -1.0, None, ALU.mult).then_inc(s["ve2"].h, 1)
                rec(("ve2p", pi), s["ve2"].inc(1))
                v.drain()
                for i in range(2):
                    t = t0 + i
                    sl = t % 2
                    if t >= 2:
                        v.wait_ge(s["h2t"].h, tok[("h2t", t - 2)])
                    v.tensor_scalar(h_tm[:, 1024 * sl:1024 * (sl + 1)],
                                    x_sb[:, 1024 * t:1024 * (t + 1)],
                                    negm[:, so2 + i:so2 + i + 1],
                                    None, ALU.add).then_inc(s["h2"].h, 1)
                    rec(("h2", t), s["h2"].inc(1))
            on("vector", resid1_pair)

            def ln2_pair(sc, t0=t0, pi=pi, so2=so2):
                sc.wait_ge(s["ve2"].h, tok[("ve2p", pi)])
                sc.activation(lnv[:, so2:so2 + 2], veps[:, so2:so2 + 2], AF.Ln)
                sc.drain()
                sc.activation(rstd[:, so2:so2 + 2], lnv[:, so2:so2 + 2], AF.Exp,
                              scale=-0.5).then_inc(s["rsd2"].h, 1)
                rec(("rsd2", pi), s["rsd2"].inc(1))
            if j2 == 0 or c + 1 >= NC:
                on("scalar", ln2_pair)
                if c + 1 < NC:
                    on("scalar", lambda sc, c=c: _st2_pq_evac(sc, c + 1, range(0, 4)))
            else:
                on("scalar", lambda sc, c=c: _st2_pq_evac(sc, c + 1, range(4, 8)))
                on("scalar", ln2_pair)

        for tw in range(TPC):
            t = 4 * c + tw
            sl = t % 2

            def h2_transpose(te, t=t, sl=sl):
                te.wait_ge(s["h2"].h, tok[("h2", t)])
                if t >= 2:
                    te.wait_ge(s["h2fe"].h, tok[("h2fe", t - 2)])
                else:
                    te.wait_ge(s["eae"].h, tok[("eae", 0, 1)])
                dst = pT[:, 1024 * sl:1024 * (sl + 1)]
                for k in range(KD):
                    ins = te.matmul(dst[:, 128 * k:128 * (k + 1)],
                                    h_tm[:, 1024 * sl + 128 * k:1024 * sl + 128 * (k + 1)],
                                    ident_sb[:], start=(k == 0), stop=(k == KD - 1),
                                    is_transpose=True)
                ins.then_inc(s["h2t"].h, 1)
                rec(("h2t", t), s["h2t"].inc(1))
            on("tensor", h2_transpose)

        for tw in range(TPC):
            t = 4 * c + tw
            sl = t % 2

            def h2fm_evac(v, t=t, tw=tw, c=c, sl=sl):
                v.wait_ge(s["h2t"].h, tok[("h2t", t)])
                if c >= 2:
                    v.wait_ge(s["ffp"].h, tok[("ffp", 4 * (c - 2) + 3)])
                src = pT[:, 1024 * sl:1024 * (sl + 1)]
                dst = (h_fm[:, 4096 * (c % 2):4096 * (c % 2 + 1)]
                       .rearrange("p (k q) -> p k q", k=KD)[:, :, 128 * tw:128 * (tw + 1)])
                v.tensor_copy(dst, src.rearrange("p (k q) -> p k q", k=KD)
                              ).then_inc(s["h2fe"].h, 1)
                rec(("h2fe", t), s["h2fe"].inc(1))
            on("vector", h2fm_evac)

        def ffn_mm(te, c=c):
            if c == 0:
                te.wait_ge(s["sgn"].h, tok[("sgn", KD - 1)])
            for tw in range(TPC):
                t = 4 * c + tw
                bank = pwA if tw % 2 == 0 else pwB
                te.wait_ge(s["h2fe"].h, tok[("h2fe", t)])
                if tw < 2:
                    te.wait_ge(s["r1"].h, tok[("r1", t + 2)])
                else:
                    te.wait_ge(s["r2"].h, tok[("r2", t - 2)])
                for j in range(2):
                    for k in range(KD):
                        ins = te.matmul(bank[:, 512 * j:512 * (j + 1)],
                                        h_fm[:, 4096 * (c % 2) + 512 * k + 128 * tw:
                                             4096 * (c % 2) + 512 * k + 128 * (tw + 1)],
                                        wslot_b[:, 1024 * k + 512 * j:
                                                1024 * k + 512 * (j + 1)],
                                        start=(k == 0), stop=(k == KD - 1))
                ins.then_inc(s["ffp"].h, 1)
                rec(("ffp", t), s["ffp"].inc(1))
        on("tensor", ffn_mm)

        for tw in range(TPC):
            t = 4 * c + tw

            def resid2(v, t=t, tw=tw):
                bank = pwA if tw % 2 == 0 else pwB
                v.wait_ge(s["ffp"].h, tok[("ffp", t)])
                v.wait_ge(s["rsd2"].h, tok[("rsd2", t // 2)])
                col = 2 * ((t // 2) % 2) + (t % 2)
                xs = x_sb[:, 1024 * t:1024 * (t + 1)]
                v.scalar_tensor_tensor(xs, bank[:], rstd[:, col:col + 1], xs,
                                       ALU.mult, ALU.add).then_inc(s["r2"].h, 1)
                rec(("r2", t), s["r2"].inc(1))
            on("vector", resid2)

            def store(eng, t=t):
                eng.wait_ge(s["r2"].h, tok[("r2", t)])
                eng.dma_start(out=y_out[128 * t:128 * (t + 1), :],
                              in_=x_sb[:, 1024 * t:1024 * (t + 1)]
                              ).then_inc(s["st%d" % (t % 2)].h, 16)
                rec(("st", t), s["st%d" % (t % 2)].inc(16))
            on("sync", store)

    def final_wait(eng):
        eng.wait_ge(s["st0"].h, tok[("st", NT - 2)])
        eng.wait_ge(s["st1"].h, tok[("st", NT - 1)])
    on("sync", final_wait)

    es.close()
    return nc


_NC_CACHE = None


def kernel(L0_attn_x, L0_attn_W_emb, L0_attn_W_anchor, L0_attn_W_out,
           L0_attn_b_out, L0_ffn_weight):
    global _NC_CACHE
    x = np.ascontiguousarray(np.asarray(L0_attn_x, dtype=np.float32))
    W_emb = np.asarray(L0_attn_W_emb, dtype=np.float32)
    W_anchor = np.asarray(L0_attn_W_anchor, dtype=np.float32)
    W_out = np.asarray(L0_attn_W_out, dtype=np.float32)
    b_out = np.asarray(L0_attn_b_out, dtype=np.float32)
    W_ffn = np.asarray(L0_ffn_weight, dtype=np.float32)

    wqkT = np.ascontiguousarray(W_emb[:1024].T)
    wvT = np.ascontiguousarray(W_emb[1024:].T)
    wanchorT = np.ascontiguousarray(W_anchor.T)
    woutT = np.ascontiguousarray(W_out.T)
    wffnT = np.ascontiguousarray(W_ffn.T)
    boutr = np.ascontiguousarray(b_out.reshape(1, 1024))
    ident = np.eye(128, dtype=np.float32)

    if _NC_CACHE is None:
        _NC_CACHE = build_nc()
    nc = _NC_CACHE

    in_maps = []
    for c in range(8):
        b, half = c // 2, c % 2
        xs = np.ascontiguousarray(x[b, half * T:(half + 1) * T, :])
        in_maps.append({"x": xs, "wqkT": wqkT, "wvT": wvT, "woutT": woutT,
                        "wffnT": wffnT, "wanchorT": wanchorT, "bout": boutr,
                        "identin": ident})
    res = run_bass_kernel_spmd(nc, in_maps, core_ids=list(range(8)))
    out = np.empty((4, 4096, 1024), dtype=np.float32)
    for c in range(8):
        b, half = c // 2, c % 2
        out[b, half * T:(half + 1) * T, :] = res.results[c]["y"]
    return out


# version 29
# speedup vs baseline: 1.0715x; 1.0715x over previous
import numpy as np
import concourse.bass as bass
import concourse.mybir as mybir
from concourse.bass_utils import run_bass_kernel_spmd
from contextlib import ExitStack

F32 = mybir.dt.float32
BF16 = mybir.dt.bfloat16
AF = mybir.ActivationFunctionType
ALU = mybir.AluOpType
AX = mybir.AxisListType

T = 2048
NT = 16
NC = 4
TPC = 4
D = 1024
KD = 8
NP = 8
EPS = 1e-5
SCALE = 0.125


class Sem:
    def __init__(self, nc, es, name):
        self.h = es.enter_context(nc.semaphore(name))
        self.n = 0

    def inc(self, k=1):
        self.n += k
        return self.n


def build_nc():
    nc = bass.Bass("TRN2", target_bir_lowering=False)

    x_in = nc.dram_tensor("x", [T, D], F32, kind="ExternalInput").ap()
    wqkT = nc.dram_tensor("wqkT", [D, D], F32, kind="ExternalInput").ap()
    wvT = nc.dram_tensor("wvT", [D, D], F32, kind="ExternalInput").ap()
    woutT = nc.dram_tensor("woutT", [D, D], F32, kind="ExternalInput").ap()
    wffnT = nc.dram_tensor("wffnT", [D, D], F32, kind="ExternalInput").ap()
    wanchorT = nc.dram_tensor("wanchorT", [64, 64], F32, kind="ExternalInput").ap()
    bout = nc.dram_tensor("bout", [1, D], F32, kind="ExternalInput").ap()
    ident_in = nc.dram_tensor("identin", [128, 128], F32, kind="ExternalInput").ap()
    y_out = nc.dram_tensor("y", [T, D], F32, kind="ExternalOutput").ap()
    cc_in = nc.dram_tensor("cc_in", [128, 520], F32, kind="Internal")
    cc_out = nc.dram_tensor("cc_out", [128, 520], F32, kind="Internal")
    cc2_in = nc.dram_tensor("cc2_in", [128, 520], F32, kind="Internal")
    cc2_out = nc.dram_tensor("cc2_out", [128, 520], F32, kind="Internal")
    ccw_in = nc.dram_tensor("ccw_in", [128, 8], F32, kind="Internal")
    ccw_out = nc.dram_tensor("ccw_out", [128, 8], F32, kind="Internal")

    es = ExitStack()
    sb = lambda n, s_, d: es.enter_context(nc.sbuf_tensor(n, s_, d))
    psum = lambda n, s_, d: es.enter_context(nc.psum_tensor(n, s_, d))

    x_sb = sb("x_sb", [128, NT * 1024], F32)
    h_tm = sb("h_tm", [128, 4 * 1024], BF16)
    stats = sb("stats", [128, 8 * 12], F32)
    mv = sb("mv", [128, 8 * 2], F32)
    veps = sb("veps", [128, 8], F32)
    lnv = sb("lnv", [128, 8], F32)
    negm = sb("negm", [128, 8], F32)
    rstd = sb("rstd", [128, 8], F32)
    nmr = sb("nmr", [128, 8], F32)
    h_fm = sb("h_fm", [128, 2 * 4096], BF16)
    wslot_a = sb("wslot_a", [128, 8192], BF16)
    wslot_b = sb("wslot_b", [128, 8192], BF16)
    qk_sb = sb("qk_sb", [128, 2 * 4096], BF16)
    e_am = sb("e_am", [128, 8 * 512], BF16)
    eh_tm = sb("eh_tm", [128, NT * 1024], BF16)
    rs_sb = sb("rs_sb", [128, 64], F32)
    recip_sb = sb("recip_sb", [128, 64], F32)
    v_aug = sb("v_aug", [128, 8 * 1040], BF16)
    agg_sb = sb("agg_sb", [128, 1040], F32)
    agg_sb2 = h_tm[:, 0:2080].bitcast(F32)
    av_blk = sb("av_blk", [128, 1024], BF16)
    anchor_blk = sb("anchor_blk", [128, 128], BF16)
    ident_sb = sb("ident_sb", [128, 128], BF16)
    bias_sb = sb("bias_sb", [1, 1024], BF16)
    ones_row = sb("ones_row", [1, 128], BF16)
    wffn_stage = sb("wffn_stage", [128, 1024], F32)
    final_sb = v_aug
    sguard = sb("sguard", [128, 32], F32)

    pT = psum("pT", [128, 2048], BF16)
    pQ = psum("pQ", [128, 1024], F32)
    pwA = psum("pwA", [128, 1024], F32)
    pwB = psum("pwB", [128, 1024], F32)

    blk = es.enter_context(nc.Block())

    def on(eng, f):
        getattr(blk, eng)(f)

    s = {}
    for name in (["wv%d" % k for k in range(8)] + ["wqk%d" % k for k in range(8)]
                 + ["ve0", "rsd0", "x0", "x0b", "x1", "x2", "x3", "wsm", "wqk", "wv", "wout",
                 "wf0", "wf1", "st0", "st1", "init", "ve", "h", "ht",
                 "hfe", "qkp", "qke", "vp", "vev", "scp", "ee", "etb",
                 "norm", "agp", "age", "sgn", "ccin", "cc",
                 "ccout", "av", "eat", "eae", "bcp", "bce", "wop", "r1", "rsd", "rsd2", "algn",
                 "ve2", "h2", "h2t", "h2fe", "ffp", "r2"]):
        s[name] = Sem(nc, es, "s_" + name)

    tok = {}
    for c in range(NC):
        tok[("x", c)] = 16
    tok["init"] = 1
    tok["ve0"] = 1
    tok["rsd0"] = 1
    tok["wsm"] = 16 * 4
    tok["wqk"] = 16 * KD
    tok["wv"] = 16 * KD
    tok["wout"] = 16 * KD
    for t in range(NT):
        for nm_ in ["h", "ht", "hfe", "etb", "norm",
                    "wop", "r1", "ve2", "h2", "h2t", "h2fe",
                    "ffp", "r2"]:
            tok[(nm_, t)] = t + 1
        tok[("st", t)] = 16 * (t // 2 + 1)

    for m in range(NC):
        for p in range(NP):
            for tw in range(TPC):
                tok[("agp", 4 * m + tw, p)] = 32 * m + 4 * p + tw + 1
            tok[("age", m, p)] = 8 * m + p + 1
    for c in range(NC):
        tok[("ve", c)] = c + 1
        tok[("rsd", c)] = c + 1
    for pi in range(NT // 2):
        tok[("ve2p", pi)] = pi + 1
        tok[("rsd2", pi)] = pi + 1
    for c in range(NC):
        for i in range(8):
            for nm_ in ["qkp", "qke", "vp", "vev", "scp", "ee",
                        "eat", "eae", "bcp", "bce"]:
                tok[(nm_, c, i)] = c * 8 + i + 1
        tok[("qklast", c)] = tok[("qkp", c, KD - 1)]
        tok[("vlast", c)] = tok[("vp", c, 7)]
        tok[("vevlast", c)] = tok[("vev", c, 7)]
    for j in range(KD):
        tok[("sgn", j)] = j + 1
        tok[("wf", j)] = 16 * (j + 1)
    tok["ccin"] = 48
    tok["cc"] = 1
    tok["ccout"] = 32
    for p in range(NP):
        tok[("av", p)] = p + 1

    def rec(key, val):
        assert tok.get(key, val) == val, (key, tok.get(key), val)
        tok[key] = val

    def init_sync(eng):
        eng.dma_start(
            out=x_sb[:, 0:2048].rearrange("p (a d) -> p a d", a=2),
            in_=x_in[0:256, :].rearrange("(a p) d -> p a d", p=128),
        ).then_inc(s["x0"].h, 16)
        rec(("x", 0), s["x0"].inc(16))
        eng.dma_start(
            out=x_sb[:, 2048:4096].rearrange("p (a d) -> p a d", a=2),
            in_=x_in[256:512, :].rearrange("(a p) d -> p a d", p=128),
        ).then_inc(s["x0b"].h, 16)
        rec(("x0b", 0), s["x0b"].inc(16))
        for c in range(1, NC):
            eng.dma_start(
                out=x_sb[:, 4096 * c:4096 * (c + 1)].rearrange("p (a d) -> p a d", a=4),
                in_=x_in[512 * c:512 * (c + 1), :].rearrange("(a p) d -> p a d", p=128),
            ).then_inc(s["x%d" % c].h, 16)
            rec(("x", c), s["x%d" % c].inc(16))
    on("sync", init_sync)

    def init_gp(eng):
        eng.memset(ones_row[:], 1.0)
        eng.memset(v_aug[:], 1.0)
        eng.memset(av_blk[:], 0.0)
        eng.memset(anchor_blk[:], 0.0)
        eng.drain()
        eng.nop().then_inc(s["init"].h, 1)
        rec("init", s["init"].inc(1))
        eng.dma_start(out=anchor_blk[0:64, 0:64], in_=wanchorT).then_inc(s["wsm"].h, 16)
        s["wsm"].inc(16)
        eng.dma_start(out=anchor_blk[64:128, 64:128], in_=wanchorT).then_inc(s["wsm"].h, 16)
        s["wsm"].inc(16)
        eng.dma_start(out=ident_sb[:], in_=ident_in).then_inc(s["wsm"].h, 16)
        s["wsm"].inc(16)
        eng.dma_start(out=bias_sb[:], in_=bout).then_inc(s["wsm"].h, 16)
        rec("wsm", s["wsm"].inc(16))
        for k in range(KD):
            eng.dma_start(out=wslot_b[:, 1024 * k:1024 * (k + 1)],
                          in_=wvT[128 * k:128 * (k + 1), :]).then_inc(s["wv%d" % k].h, 16)
            s["wv%d" % k].inc(16)
        for k in range(KD):
            eng.dma_start(out=wslot_a[:, 1024 * k:1024 * (k + 1)],
                          in_=wqkT[128 * k:128 * (k + 1), :]).then_inc(s["wqk%d" % k].h, 16)
            s["wqk%d" % k].inc(16)
        eng.memset(sguard[:, 24:32], 0.0)
        eng.drain()
        eng.dma_start(out=ccw_in.ap(), in_=sguard[:, 24:32]).then_inc(s["ccin"].h, 16)
        s["ccin"].inc(16)
        eng.wait_ge(s["ccin"].h, 16)
        eng.collective_compute("AllReduce", ALU.add,
                               [[0, 1], [2, 3], [4, 5], [6, 7]],
                               ins=[ccw_in.ap()], outs=[ccw_out.ap()]
                               ).then_inc(s["init"].h, 1)
        s["init"].inc(1)
        eng.collective_compute("AllReduce", ALU.add,
                               [[0, 1], [2, 3], [4, 5], [6, 7]],
                               ins=[ccw_in.ap()], outs=[ccw_out.ap()]
                               ).then_inc(s["algn"].h, 1)
        s["algn"].inc(1)
    on("gpsimd", init_gp)


    def emit_A_statsA(c, scalar_too=True):
        so = 4 * (c % 2)

        def ln_stats(v, c=c, so=so):
            if c == 3:
                v.wait_ge(s["algn"].h, 1)
            v.wait_ge(s["x%d" % c].h, tok[("x", c)])
            if c >= 2:
                v.wait_ge(s["h"].h, tok[("h", 4 * (c - 2) + 3)])
            tws = list(range(TPC))
            if c == 0:
                xs = x_sb[:, 0:1024]
                v.bn_stats(stats[:, 12 * so:12 * so + 6], xs[:, 0:512])
                v.bn_stats(stats[:, 12 * so + 6:12 * so + 12], xs[:, 512:1024])
                v.drain()
                v.bn_aggr(mv[:, 2 * so:2 * so + 2], stats[:, 12 * so:12 * so + 12])
                v.drain()
                v.tensor_scalar(veps[:, so:so + 1], mv[:, 2 * so + 1:2 * so + 2],
                                EPS, None, ALU.add)
                v.tensor_scalar(negm[:, so:so + 1], mv[:, 2 * so:2 * so + 1],
                                -1.0, None, ALU.mult).then_inc(s["ve0"].h, 1)
                rec("ve0", s["ve0"].inc(1))
                tws = [1, 2, 3]
            for tw in tws:
                if c == 0 and tw == 2:
                    v.wait_ge(s["x0b"].h, tok[("x0b", 0)])
                t = 4 * c + tw
                xs = x_sb[:, 1024 * t:1024 * (t + 1)]
                v.bn_stats(stats[:, 12 * (so + tw):12 * (so + tw) + 6], xs[:, 0:512])
                v.bn_stats(stats[:, 12 * (so + tw) + 6:12 * (so + tw) + 12],
                           xs[:, 512:1024])
            v.drain()
            for tw in tws:
                v.bn_aggr(mv[:, 2 * (so + tw):2 * (so + tw) + 2],
                          stats[:, 12 * (so + tw):12 * (so + tw) + 12])
            v.drain()
            lo = tws[0]
            mvv = mv[:, 2 * (so + lo):2 * so + 8].rearrange("p (a b) -> p a b", b=2)
            v.tensor_scalar(veps[:, so + lo:so + 4], mvv[:, :, 1:2].squeeze(2),
                            EPS, None, ALU.add)
            v.tensor_scalar(negm[:, so + lo:so + 4], mvv[:, :, 0:1].squeeze(2),
                            -1.0, None, ALU.mult).then_inc(s["ve"].h, 1)
            rec(("ve", c), s["ve"].inc(1))
        on("vector", ln_stats)

        def ln_scalar(sc, c=c, so=so):
            if c == 3:
                sc.wait_ge(s["algn"].h, 1)
            lo = 0
            if c == 0:
                sc.wait_ge(s["ve0"].h, tok["ve0"])
                sc.activation(lnv[:, so:so + 1], veps[:, so:so + 1], AF.Ln)
                sc.drain()
                sc.activation(rstd[:, so:so + 1], lnv[:, so:so + 1], AF.Exp,
                              scale=-0.5).then_inc(s["rsd0"].h, 1)
                rec("rsd0", s["rsd0"].inc(1))
                lo = 1
            sc.wait_ge(s["ve"].h, tok[("ve", c)])
            sc.activation(lnv[:, so + lo:so + 4], veps[:, so + lo:so + 4], AF.Ln)
            sc.drain()
            sc.activation(rstd[:, so + lo:so + 4], lnv[:, so + lo:so + 4], AF.Exp,
                          scale=-0.5).then_inc(s["rsd"].h, 1)
            rec(("rsd", c), s["rsd"].inc(1))
        if scalar_too:
            on("scalar", ln_scalar)
        else:
            emit_A_statsA.deferred = ln_scalar

    def emit_A_statsB(c):
        so = 4 * (c % 2)

        def ln_apply(v, c=c, so=so):
            lo = 0
            if c == 0:
                v.wait_ge(s["rsd0"].h, tok["rsd0"])
                v.scalar_tensor_tensor(nmr[:, so:so + 1], negm[:, so:so + 1], 1.0,
                                       rstd[:, so:so + 1], ALU.mult, ALU.mult)
                v.drain()
                v.tensor_scalar(h_tm[:, 0:1024], x_sb[:, 0:1024],
                                rstd[:, so:so + 1], nmr[:, so:so + 1],
                                ALU.mult, ALU.add).then_inc(s["h"].h, 1)
                rec(("h", 0), s["h"].inc(1))
                lo = 1
            v.wait_ge(s["rsd"].h, tok[("rsd", c)])
            v.scalar_tensor_tensor(nmr[:, so + lo:so + 4], negm[:, so + lo:so + 4],
                                   1.0, rstd[:, so + lo:so + 4],
                                   ALU.mult, ALU.mult)
            v.drain()
            for tw in range(lo, TPC):
                t = 4 * c + tw
                if c >= 1 and tw == 0:
                    v.wait_ge(s["ht"].h, tok[("ht", 4 * (c - 1) + 3)])
                v.tensor_scalar(h_tm[:, 1024 * tw:1024 * (tw + 1)],
                                x_sb[:, 1024 * t:1024 * (t + 1)],
                                rstd[:, so + tw:so + tw + 1],
                                nmr[:, so + tw:so + tw + 1],
                                ALU.mult, ALU.add).then_inc(s["h"].h, 1)
                rec(("h", t), s["h"].inc(1))
        on("vector", ln_apply)

    def emit_A1(c):
        for tw in range(TPC):
            t = 4 * c + tw
            sl = t % 2

            def h_transpose(te, t=t, sl=sl, c=c, tw=tw):
                if c == 3 and tw == 0:
                    te.wait_ge(s["algn"].h, 1)
                te.wait_ge(s["h"].h, tok[("h", t)])
                te.wait_ge(s["wsm"].h, tok["wsm"])
                if t >= 2:
                    te.wait_ge(s["hfe"].h, tok[("hfe", t - 2)])
                dst = pT[:, 1024 * sl:1024 * (sl + 1)]
                for k in range(KD):
                    ins = te.matmul(dst[:, 128 * k:128 * (k + 1)],
                                    h_tm[:, 1024 * tw + 128 * k:1024 * tw + 128 * (k + 1)],
                                    ident_sb[:], start=(k == 0), stop=(k == KD - 1),
                                    is_transpose=True)
                ins.then_inc(s["ht"].h, 1)
                rec(("ht", t), s["ht"].inc(1))
            on("tensor", h_transpose)

            def hfm_evac(v, t=t, tw=tw, c=c, sl=sl):
                v.wait_ge(s["ht"].h, tok[("ht", t)])
                if c >= 2:
                    v.wait_ge(s["vp"].h, tok[("vlast", c - 2)])
                    v.wait_ge(s["qkp"].h, tok[("qklast", c - 2)])
                src = pT[:, 1024 * sl:1024 * (sl + 1)]
                dst = (h_fm[:, 4096 * (c % 2):4096 * (c % 2 + 1)]
                       .rearrange("p (k q) -> p k q", k=KD)[:, :, 128 * tw:128 * (tw + 1)])
                v.tensor_copy(dst, src.rearrange("p (k q) -> p k q", k=KD)
                              ).then_inc(s["hfe"].h, 1)
                rec(("hfe", t), s["hfe"].inc(1))
            on("vector", hfm_evac)

    def emit_A2(c):
        def emb_v(te, c=c):
            for tw in range(TPC):
                te.wait_ge(s["hfe"].h, tok[("hfe", 4 * c + tw)])
                for vo in range(2):
                    g = 2 * tw + vo
                    half = g % 2
                    seq = c * 8 + g
                    if seq >= 2:
                        gg = seq - 2
                        te.wait_ge(s["vev"].h, tok[("vev", gg // 8, gg % 8)])
                    if c >= 2 and g < 2:
                        te.wait_ge(s["norm"].h, tok[("norm", 4 * (c - 2) + g)])
                    for k in range(KD):
                        if c == 0 and tw == 0:
                            te.wait_ge(s["wv%d" % k].h, 16)
                        ins = te.matmul(pwA[:, 512 * half:512 * (half + 1)],
                                        h_fm[:, 4096 * (c % 2) + 512 * k + 128 * tw:
                                             4096 * (c % 2) + 512 * k + 128 * (tw + 1)],
                                        wslot_b[:, 1024 * k + 512 * vo:
                                                1024 * k + 512 * (vo + 1)],
                                        start=(k == 0), stop=(k == KD - 1))
                    ins.then_inc(s["vp"].h, 1)
                    rec(("vp", c, g), s["vp"].inc(1))
        on("tensor", emb_v)

        def v_evac(v, c=c):
            for tw in range(TPC):
                t = 4 * c + tw
                tsl = t % 8
                for vo in range(2):
                    g = 2 * tw + vo
                    half = g % 2
                    v.wait_ge(s["vp"].h, tok[("vp", c, g)])
                    if t >= 8:
                        v.wait_ge(s["agp"].h, tok[("agp", t - 8, NP - 1)])
                    if t == 0 and vo == 0:
                        v.wait_ge(s["init"].h, tok["init"])
                    src = pwA[:, 512 * half:512 * (half + 1)].rearrange(
                        "p (q h d) -> p q h d", q=4, h=2)
                    base = v_aug[:, 1040 * tsl + 520 * vo:1040 * tsl + 520 * (vo + 1)]
                    dst = (base.rearrange("p (q r) -> p q r", q=4)
                           .rearrange("p q (h e) -> p q h e", h=2)[:, :, :, 0:64])
                    v.tensor_copy(dst, src).then_inc(s["vev"].h, 1)
                    rec(("vev", c, g), s["vev"].inc(1))
        on("vector", v_evac)

        def emb_qk(te, c=c):
            for o in range(KD):
                half = o % 2
                seq = c * KD + o
                if seq >= 2:
                    oo = seq - 2
                    te.wait_ge(s["qke"].h, tok[("qke", oo // KD, oo % KD)])
                if c >= 2 and o < 2:
                    te.wait_ge(s["age"].h, tok[("age", c - 2, 6 + o)])
                for k in range(KD):
                    if c == 0 and o == 0:
                        te.wait_ge(s["wqk%d" % k].h, 16)
                    ins = te.matmul(pQ[:, 512 * half:512 * (half + 1)],
                                    wslot_a[:, 1024 * k + 128 * o:1024 * k + 128 * (o + 1)],
                                    h_fm[:, 4096 * (c % 2) + 512 * k:
                                         4096 * (c % 2) + 512 * (k + 1)],
                                    start=(k == 0), stop=(k == KD - 1))
                ins.then_inc(s["qkp"].h, 1)
                rec(("qkp", c, o), s["qkp"].inc(1))
        on("tensor", emb_qk)

        def qk_evac(v, c=c):
            for o in range(KD):
                half = o % 2
                v.wait_ge(s["qkp"].h, tok[("qkp", c, o)])
                if c >= 2:
                    v.wait_ge(s["scp"].h, tok[("scp", c - 2, o)])
                v.tensor_copy(qk_sb[:, 4096 * (c % 2) + 512 * o:
                                    4096 * (c % 2) + 512 * (o + 1)],
                              pQ[:, 512 * half:512 * (half + 1)]).then_inc(s["qke"].h, 1)
                rec(("qke", c, o), s["qke"].inc(1))
        on("vector", qk_evac)

    def emit_M(m):
        et_banks = [pwA[:, 0:512].bitcast(BF16), pwA[:, 512:1024].bitcast(BF16),
                    pwB[:, 0:512].bitcast(BF16), pwB[:, 512:1024].bitcast(BF16)]

        def te_middle(te, m=m):
            for p in range(NP):
                half = p % 2
                if p < 2:
                    if m < NC - 1:
                        te.wait_ge(s["qke"].h, tok[("qke", m + 1, 6 + p)])
                    else:
                        te.wait_ge(s["age"].h, tok[("age", m - 1, 6 + p)])
                else:
                    te.wait_ge(s["ee"].h, tok[("ee", m, p - 2)])
                te.wait_ge(s["qke"].h, tok[("qke", m, p)])
                te.matmul(pQ[:, 512 * half:512 * (half + 1)], anchor_blk[:],
                          qk_sb[:, 4096 * (m % 2) + 512 * p:4096 * (m % 2) + 512 * (p + 1)],
                          start=True, stop=True).then_inc(s["scp"].h, 1)
                rec(("scp", m, p), s["scp"].inc(1))
            for tw in range(TPC):
                t = 4 * m + tw
                if tw < 2:
                    if m < NC - 1:
                        te.wait_ge(s["vev"].h, tok[("vev", m + 1, 6 + tw)])
                    else:
                        te.wait_ge(s["norm"].h, tok[("norm", 4 * (m - 1) + tw)])
                else:
                    if m >= 1:
                        te.wait_ge(s["norm"].h, tok[("norm", 4 * (m - 1) + tw)])
                for p in range(NP):
                    te.wait_ge(s["ee"].h, tok[("ee", m, p)])
                    ins = te.matmul(et_banks[tw][:, 128 * p:128 * (p + 1)],
                                    e_am[:, 512 * p + 128 * tw:512 * p + 128 * (tw + 1)],
                                    ident_sb[:], start=(p == 0), stop=(p == NP - 1),
                                    is_transpose=True)
                ins.then_inc(s["etb"].h, 1)
                rec(("etb", t), s["etb"].inc(1))
            for p in range(NP):
                half = p % 2
                if p < 2:
                    te.wait_ge(s["ee"].h, tok[("ee", m, 6 + p)])
                else:
                    te.wait_ge(s["age"].h, tok[("age", m, p - 2)])
                for tw in range(TPC):
                    t = 4 * m + tw
                    tsl = t % 8
                    if p == 0:
                        te.wait_ge(s["norm"].h, tok[("norm", t)])
                    te.matmul(pQ[:, 512 * half:512 * half + 130],
                              eh_tm[:, 1024 * t + 128 * p:1024 * t + 128 * (p + 1)],
                              v_aug[:, 1040 * tsl + 130 * p:1040 * tsl + 130 * (p + 1)],
                              start=(tw == 0), stop=(tw == TPC - 1)
                              ).then_inc(s["agp"].h, 1)
                    rec(("agp", t, p), s["agp"].inc(1))
        on("tensor", te_middle)

        def exp_evac(sc, m=m):
            for p in range(NP):
                half = p % 2
                sc.wait_ge(s["scp"].h, tok[("scp", m, p)])
                if m >= 1:
                    sc.wait_ge(s["etb"].h, tok[("etb", 4 * (m - 1) + 3)])
                sc.activation(e_am[:, 512 * p:512 * (p + 1)],
                              pQ[:, 512 * half:512 * (half + 1)],
                              AF.Exp, scale=SCALE).then_inc(s["ee"].h, 1)
                rec(("ee", m, p), s["ee"].inc(1))
        on("scalar", exp_evac)

        def dve_middle(v, m=m):
            def emit_norm_pair(tw0):
                for tw in (tw0, tw0 + 1):
                    t = 4 * m + tw
                    v.wait_ge(s["etb"].h, tok[("etb", t)])
                    src4 = et_banks[tw].rearrange("p (g h a) -> p g h a", g=NP, h=2)
                    v.tensor_reduce(rs_sb[:, 16 * tw:16 * (tw + 1)]
                                    .rearrange("p (g h) -> p g h", g=NP).unsqueeze(3),
                                    src4, AX.X, ALU.add)
                v.drain()
                v.reciprocal(recip_sb[:, 16 * tw0:16 * (tw0 + 2)],
                             rs_sb[:, 16 * tw0:16 * (tw0 + 2)])
                v.drain()
                for tw in (tw0, tw0 + 1):
                    t = 4 * m + tw
                    src4 = et_banks[tw].rearrange("p (g h a) -> p g h a", g=NP, h=2)
                    rb = (recip_sb[:, 16 * tw:16 * (tw + 1)]
                          .rearrange("p (g h) -> p g h", g=NP).unsqueeze(3)
                          .broadcast_to([128, NP, 2, 64]))
                    v.tensor_tensor(eh_tm[:, 1024 * t:1024 * (t + 1)]
                                    .rearrange("p (g h a) -> p g h a", g=NP, h=2),
                                    src4, rb, ALU.mult).then_inc(s["norm"].h, 1)
                    rec(("norm", t), s["norm"].inc(1))

            emit_norm_pair(0)
            emit_norm_pair(2)
            for p in range(NP):
                half = p % 2
                v.wait_ge(s["agp"].h, tok[("agp", 4 * m + 3, p)])
                src = pQ[:, 512 * half:512 * half + 130]
                buf = agg_sb if m < 2 else agg_sb2
                dst = buf[:, 130 * p:130 * (p + 1)]
                if m in (0, 2):
                    ins = v.tensor_copy(dst, src)
                else:
                    ins = v.tensor_tensor(dst, dst, src, ALU.add)
                ins.then_inc(s["age"].h, 1)
                rec(("age", m, p), s["age"].inc(1))
        on("vector", dve_middle)

    emit_A_statsA(0)
    emit_A_statsB(0)
    emit_A1(0)
    emit_A_statsA(1)
    emit_A2(0)
    emit_A_statsB(1)
    emit_A1(1)
    emit_A_statsA(2)
    emit_A2(1)
    emit_A_statsB(2)
    emit_A1(2)
    emit_A_statsA(3, scalar_too=False)
    emit_M(0)
    on("scalar", emit_A_statsA.deferred)
    emit_A2(2)
    emit_A_statsB(3)
    emit_A1(3)
    emit_M(1)
    emit_A2(3)
    emit_M(2)
    emit_M(3)

    def late_weights(eng):
        eng.wait_ge(s["qkp"].h, tok[("qklast", NC - 1)])
        for k in range(KD):
            eng.dma_start(out=wslot_a[:, 1024 * k:1024 * (k + 1)],
                          in_=woutT[128 * k:128 * (k + 1), :]).then_inc(s["wout"].h, 16)
            s["wout"].inc(16)
    on("gpsimd", late_weights)

    def wffn_loads(eng):
        for j in range(KD):
            if j >= 1:
                eng.wait_ge(s["sgn"].h, tok[("sgn", j - 1)])
            eng.dma_start(out=wffn_stage[:],
                          in_=wffnT[128 * j:128 * (j + 1), :]).then_inc(s["wf0"].h, 16)
            rec(("wf", j), s["wf0"].inc(16))
    on("sync", wffn_loads)

    def sign_ffn(sc):
        sc.wait_ge(s["vp"].h, tok[("vlast", NC - 1)])
        for j in range(KD):
            sc.wait_ge(s["wf0"].h, tok[("wf", j)])
            sc.sign(wslot_b[:, 1024 * j:1024 * (j + 1)],
                    wffn_stage[:]).then_inc(s["sgn"].h, 1)
            rec(("sgn", j), s["sgn"].inc(1))
    on("scalar", sign_ffn)

    def collective(eng):
        def compact(dst_dram, sb_src):
            top = sb_src[0:64, :].rearrange("p (g r) -> p g r", g=NP)[:, :, 0:65]
            bot = sb_src[64:128, :].rearrange("p (g r) -> p g r", g=NP)[:, :, 65:130]
            eng.dma_start(out=dst_dram.ap()[0:64, :].rearrange("p (g r) -> p g r", g=NP),
                          in_=top).then_inc(s["ccin"].h, 16)
            s["ccin"].inc(16)
            eng.dma_start(out=dst_dram.ap()[64:128, :].rearrange("p (g r) -> p g r", g=NP),
                          in_=bot).then_inc(s["ccin"].h, 16)
            s["ccin"].inc(16)

        eng.wait_ge(s["age"].h, tok[("age", 1, NP - 1)])
        compact(cc_in, agg_sb[:])
        eng.wait_ge(s["ccin"].h, s["ccin"].n)
        eng.collective_compute("AllReduce", ALU.add,
                               [[0, 1], [2, 3], [4, 5], [6, 7]],
                               ins=[cc_in.ap()], outs=[cc_out.ap()]
                               ).then_inc(s["cc"].h, 1)
        s["cc"].inc(1)
        eng.wait_ge(s["age"].h, tok[("age", NC - 1, NP - 1)])
        compact(cc2_in, agg_sb2[:, 0:1040])
        eng.wait_ge(s["ccin"].h, s["ccin"].n)
        eng.collective_compute("AllReduce", ALU.add,
                               [[0, 1], [2, 3], [4, 5], [6, 7]],
                               ins=[cc2_in.ap()], outs=[cc2_out.ap()]
                               ).then_inc(s["cc"].h, 1)
        s["cc"].inc(1)
        eng.wait_ge(s["cc"].h, s["cc"].n)
        eng.dma_start(out=agg_sb[:, 0:520], in_=cc_out.ap()).then_inc(s["ccout"].h, 16)
        s["ccout"].inc(16)
        eng.dma_start(out=agg_sb[:, 520:1040], in_=cc2_out.ap()).then_inc(s["ccout"].h, 16)
        rec("ccout", s["ccout"].inc(16))
    on("gpsimd", collective)

    def av_prep(v):
        v.wait_ge(s["ccout"].h, tok["ccout"])
        v.tensor_tensor(agg_sb[:, 0:520], agg_sb[:, 0:520], agg_sb[:, 520:1040],
                        ALU.add)
        v.drain()
        scol = agg_sb[:, 0:520].rearrange("p (g r) -> p g r", g=NP)[:, :, 64:65].squeeze(2)
        iszero = sguard[:, 0:8]
        ssafe = sguard[:, 8:16]
        rg = sguard[:, 24:32]
        v.tensor_single_scalar(iszero, scol, 0.0, ALU.is_equal)
        v.scalar_tensor_tensor(ssafe, scol, 0.0, scol, ALU.is_equal, ALU.add)
        v.drain()
        v.reciprocal(rg, ssafe)
        v.drain()
        v.tensor_tensor(rg, rg, iszero, ALU.subtract)
        v.drain()
        for p in range(NP):
            v.tensor_scalar(av_blk[0:64, 128 * p:128 * p + 64],
                            agg_sb[0:64, 65 * p:65 * p + 64],
                            rg[0:64, p:p + 1], None, ALU.mult)
            v.tensor_scalar(av_blk[64:128, 128 * p + 64:128 * p + 128],
                            agg_sb[64:128, 65 * p:65 * p + 64],
                            rg[64:128, p:p + 1], None,
                            ALU.mult).then_inc(s["av"].h, 1)
            rec(("av", p), s["av"].inc(1))
    c0_halves_bf = [pT[:, 0:1024], pT[:, 1024:2048]]
    c0_halves = [None, None, pwA[:, 0:512], pwA[:, 512:1024],
                 pwB[:, 0:512], pwB[:, 512:1024], pQ[:, 0:512], pQ[:, 512:1024]]

    def _st2_c0_evac_a(v):
        for p in range(NP):
            v.wait_ge(s["eat"].h, tok[("eat", 0, p)])
            srcp = (c0_halves_bf[p] if p < 2
                    else c0_halves[p].bitcast(BF16))[:, 0:512]
            v.tensor_copy(qk_sb[:, 512 * p:512 * (p + 1)],
                          srcp).then_inc(s["eae"].h, 1)
            rec(("eae", 0, p), s["eae"].inc(1))
    on("vector", _st2_c0_evac_a)
    on("vector", av_prep)

    def _st2_c0(te):
        for p in range(NP):
            if p < 2:
                te.wait_ge(s["hfe"].h, tok[("hfe", NT - 1)])
            elif p < 6:
                te.wait_ge(s["norm"].h, tok[("norm", 12 + (p - 2))])
            else:
                te.wait_ge(s["age"].h, tok[("age", NC - 1, p)])
            dst = c0_halves_bf[p] if p < 2 else c0_halves[p].bitcast(BF16)
            for tw in range(TPC):
                ins = te.matmul(dst[:, 128 * tw:128 * (tw + 1)],
                                eh_tm[:, 1024 * tw + 128 * p:1024 * tw + 128 * (p + 1)],
                                ident_sb[:], start=(tw == 0), stop=(tw == TPC - 1),
                                is_transpose=True)
            ins.then_inc(s["eat"].h, 1)
            rec(("eat", 0, p), s["eat"].inc(1))
        for p in range(NP):
            half = p % 2
            te.wait_ge(s["av"].h, tok[("av", p)])
            if p < 2:
                te.wait_ge(s["eae"].h, tok[("eae", 0, 6 + p)])
            else:
                te.wait_ge(s["bce"].h, tok[("bce", 0, p - 2)])
            te.wait_ge(s["eae"].h, tok[("eae", 0, p)])
            te.matmul(pQ[:, 512 * half:512 * (half + 1)],
                      av_blk[:, 128 * p:128 * (p + 1)],
                      qk_sb[:, 512 * p:512 * (p + 1)],
                      start=True, stop=True).then_inc(s["bcp"].h, 1)
            rec(("bcp", 0, p), s["bcp"].inc(1))
    def _st2_c0_evac(v):
        for p in range(NP):
            half = p % 2
            v.wait_ge(s["bcp"].h, tok[("bcp", 0, p)])
            v.tensor_copy(final_sb[:, 512 * p:512 * (p + 1)],
                          pQ[:, 512 * half:512 * (half + 1)]).then_inc(s["bce"].h, 1)
            rec(("bce", 0, p), s["bce"].inc(1))
    def _st2_pq(te, c):
        for p in range(NP):
            half = p % 2
            if p < 2:
                te.wait_ge(s["bce"].h, tok[("bce", c - 1, 6 + p)])
            else:
                te.wait_ge(s["bce"].h, tok[("bce", c, p - 2)])
            dst = pQ[:, 512 * half:512 * (half + 1)].bitcast(BF16)
            for tw in range(TPC):
                t = 4 * c + tw
                ins = te.matmul(dst[:, 128 * tw:128 * (tw + 1)],
                                eh_tm[:, 1024 * t + 128 * p:1024 * t + 128 * (p + 1)],
                                ident_sb[:], start=(tw == 0), stop=(tw == TPC - 1),
                                is_transpose=True)
            ins.then_inc(s["eat"].h, 1)
            rec(("eat", c, p), s["eat"].inc(1))
            te.wait_ge(s["eae"].h, tok[("eae", c, p)])
            te.matmul(pQ[:, 512 * half:512 * (half + 1)],
                      av_blk[:, 128 * p:128 * (p + 1)],
                      qk_sb[:, 4096 * (c % 2) + 512 * p:
                           4096 * (c % 2) + 512 * (p + 1)],
                      start=True, stop=True).then_inc(s["bcp"].h, 1)
            rec(("bcp", c, p), s["bcp"].inc(1))

    def _st2_pq_evac(sc, c, plist):
        for p in plist:
            half = p % 2
            sc.wait_ge(s["eat"].h, tok[("eat", c, p)])
            if c >= 2:
                sc.wait_ge(s["bcp"].h, tok[("bcp", c - 2, p)])
            sc.copy(qk_sb[:, 4096 * (c % 2) + 512 * p:
                          4096 * (c % 2) + 512 * (p + 1)],
                    pQ[:, 512 * half:512 * (half + 1)].bitcast(BF16)[:, 0:512]
                    ).then_inc(s["eae"].h, 1)
            rec(("eae", c, p), s["eae"].inc(1))
            sc.wait_ge(s["bcp"].h, tok[("bcp", c, p)])
            if p == 0 and c >= 2:
                sc.wait_ge(s["wop"].h, tok[("wop", 4 * (c - 2) + 3)])
            sc.copy(final_sb[:, 4096 * (c % 2) + 512 * p:
                             4096 * (c % 2) + 512 * (p + 1)],
                    pQ[:, 512 * half:512 * (half + 1)]).then_inc(s["bce"].h, 1)
            rec(("bce", c, p), s["bce"].inc(1))

    on("tensor", _st2_c0)
    on("vector", _st2_c0_evac)

    for c in range(NC):
        def wout_mm(te, c=c):
            for tw in range(TPC):
                t = 4 * c + tw
                bank = pwA if tw % 2 == 0 else pwB
                if tw >= 2:
                    te.wait_ge(s["r1"].h, tok[("r1", t - 2)])
                elif c >= 1:
                    te.wait_ge(s["r2"].h, tok[("r2", t - 2)])
                elif tw == 0:
                    te.wait_ge(s["eae"].h, tok[("eae", 0, 5)])
                    te.wait_ge(s["init"].h, tok["init"])
                    te.wait_ge(s["wsm"].h, tok["wsm"])
                    te.wait_ge(s["wout"].h, tok["wout"])
                for j in range(2):
                    te.matmul(bank[:, 512 * j:512 * (j + 1)], ones_row[:],
                              bias_sb[:, 512 * j:512 * (j + 1)], start=True, stop=False)
                for k in range(KD):
                    if tw == 0:
                        te.wait_ge(s["bce"].h, tok[("bce", c, k)])
                    for j in range(2):
                        ins = te.matmul(bank[:, 512 * j:512 * (j + 1)],
                                        final_sb[:, 4096 * (c % 2) + 512 * k + 128 * tw:
                                                4096 * (c % 2) + 512 * k + 128 * (tw + 1)],
                                        wslot_a[:, 1024 * k + 512 * j:
                                                1024 * k + 512 * (j + 1)],
                                        start=False, stop=(k == KD - 1))
                ins.then_inc(s["wop"].h, 1)
                rec(("wop", t), s["wop"].inc(1))
        on("tensor", wout_mm)
        if c + 1 < NC:
            on("tensor", lambda te, c=c: _st2_pq(te, c + 1))

        for j2 in range(2):
            t0 = 4 * c + 2 * j2
            pi = t0 // 2
            so2 = 2 * (pi % 2)

            def resid1_tile(v, t0, pi, so2, i):
                t = t0 + i
                bank = pwA if t % 2 == 0 else pwB
                v.wait_ge(s["wop"].h, tok[("wop", t)])
                xs = x_sb[:, 1024 * t:1024 * (t + 1)]
                v.tensor_tensor(xs, xs, bank[:], ALU.add).then_inc(s["r1"].h, 1)
                rec(("r1", t), s["r1"].inc(1))
                v.drain()
                if pi >= 2 and i == 0:
                    v.wait_ge(s["h2"].h, tok[("h2", t0 - 3)])
                v.bn_stats(stats[:, 12 * (so2 + i):12 * (so2 + i) + 6], xs[:, 0:512])
                v.bn_stats(stats[:, 12 * (so2 + i) + 6:12 * (so2 + i) + 12],
                           xs[:, 512:1024])
                v.drain()
                v.bn_aggr(mv[:, 2 * (so2 + i):2 * (so2 + i) + 2],
                          stats[:, 12 * (so2 + i):12 * (so2 + i) + 12])
                v.drain()
                v.tensor_scalar(veps[:, so2 + i:so2 + i + 1],
                                mv[:, 2 * (so2 + i) + 1:2 * (so2 + i) + 2],
                                EPS, None, ALU.add)
                ins = v.tensor_scalar(negm[:, so2 + i:so2 + i + 1],
                                      mv[:, 2 * (so2 + i):2 * (so2 + i) + 1],
                                      -1.0, None, ALU.mult)
                if i == 1:
                    ins.then_inc(s["ve2"].h, 1)
                    rec(("ve2p", pi), s["ve2"].inc(1))
                v.drain()
                sl = t % 2
                if t >= 2:
                    v.wait_ge(s["h2t"].h, tok[("h2t", t - 2)])
                v.tensor_scalar(h_tm[:, 1024 * sl:1024 * (sl + 1)], xs,
                                negm[:, so2 + i:so2 + i + 1],
                                None, ALU.add).then_inc(s["h2"].h, 1)
                rec(("h2", t), s["h2"].inc(1))

            def resid1_pair(v, t0=t0, pi=pi, so2=so2, j2=j2, c=c):
                if c == 3 and j2 == 0:
                    resid1_tile(v, t0, pi, so2, 0)
                    resid1_tile(v, t0, pi, so2, 1)
                    return
                for i in range(2):
                    t = t0 + i
                    bank = pwA if t % 2 == 0 else pwB
                    v.wait_ge(s["wop"].h, tok[("wop", t)])
                    xs = x_sb[:, 1024 * t:1024 * (t + 1)]
                    v.tensor_tensor(xs, xs, bank[:], ALU.add).then_inc(s["r1"].h, 1)
                    rec(("r1", t), s["r1"].inc(1))
                v.drain()
                if pi >= 2:
                    v.wait_ge(s["h2"].h, tok[("h2", t0 - 3)])
                for i in range(2):
                    t = t0 + i
                    xs = x_sb[:, 1024 * t:1024 * (t + 1)]
                    v.bn_stats(stats[:, 12 * (so2 + i):12 * (so2 + i) + 6],
                               xs[:, 0:512])
                    v.bn_stats(stats[:, 12 * (so2 + i) + 6:12 * (so2 + i) + 12],
                               xs[:, 512:1024])
                v.drain()
                for i in range(2):
                    v.bn_aggr(mv[:, 2 * (so2 + i):2 * (so2 + i) + 2],
                              stats[:, 12 * (so2 + i):12 * (so2 + i) + 12])
                v.drain()
                mvv = mv[:, 2 * so2:2 * so2 + 4].rearrange("p (a b) -> p a b", b=2)
                v.tensor_scalar(veps[:, so2:so2 + 2], mvv[:, :, 1:2].squeeze(2),
                                EPS, None, ALU.add)
                v.tensor_scalar(negm[:, so2:so2 + 2], mvv[:, :, 0:1].squeeze(2),
                                -1.0, None, ALU.mult).then_inc(s["ve2"].h, 1)
                rec(("ve2p", pi), s["ve2"].inc(1))
                v.drain()
                for i in range(2):
                    t = t0 + i
                    sl = t % 2
                    if t >= 2:
                        v.wait_ge(s["h2t"].h, tok[("h2t", t - 2)])
                    v.tensor_scalar(h_tm[:, 1024 * sl:1024 * (sl + 1)],
                                    x_sb[:, 1024 * t:1024 * (t + 1)],
                                    negm[:, so2 + i:so2 + i + 1],
                                    None, ALU.add).then_inc(s["h2"].h, 1)
                    rec(("h2", t), s["h2"].inc(1))
            on("vector", resid1_pair)

            def ln2_pair(sc, t0=t0, pi=pi, so2=so2):
                sc.wait_ge(s["ve2"].h, tok[("ve2p", pi)])
                sc.activation(lnv[:, so2:so2 + 2], veps[:, so2:so2 + 2], AF.Ln)
                sc.drain()
                sc.activation(rstd[:, so2:so2 + 2], lnv[:, so2:so2 + 2], AF.Exp,
                              scale=-0.5).then_inc(s["rsd2"].h, 1)
                rec(("rsd2", pi), s["rsd2"].inc(1))
            if j2 == 0 or c + 1 >= NC:
                on("scalar", ln2_pair)
                if c + 1 < NC:
                    on("scalar", lambda sc, c=c: _st2_pq_evac(sc, c + 1, range(0, 4)))
            else:
                on("scalar", lambda sc, c=c: _st2_pq_evac(sc, c + 1, range(4, 8)))
                on("scalar", ln2_pair)

        for tw in range(TPC):
            t = 4 * c + tw
            sl = t % 2

            def h2_transpose(te, t=t, sl=sl):
                te.wait_ge(s["h2"].h, tok[("h2", t)])
                if t >= 2:
                    te.wait_ge(s["h2fe"].h, tok[("h2fe", t - 2)])
                else:
                    te.wait_ge(s["eae"].h, tok[("eae", 0, 1)])
                dst = pT[:, 1024 * sl:1024 * (sl + 1)]
                for k in range(KD):
                    ins = te.matmul(dst[:, 128 * k:128 * (k + 1)],
                                    h_tm[:, 1024 * sl + 128 * k:1024 * sl + 128 * (k + 1)],
                                    ident_sb[:], start=(k == 0), stop=(k == KD - 1),
                                    is_transpose=True)
                ins.then_inc(s["h2t"].h, 1)
                rec(("h2t", t), s["h2t"].inc(1))
            on("tensor", h2_transpose)

        for tw in range(TPC):
            t = 4 * c + tw
            sl = t % 2

            def h2fm_evac(v, t=t, tw=tw, c=c, sl=sl):
                v.wait_ge(s["h2t"].h, tok[("h2t", t)])
                if c >= 2:
                    v.wait_ge(s["ffp"].h, tok[("ffp", 4 * (c - 2) + 3)])
                src = pT[:, 1024 * sl:1024 * (sl + 1)]
                dst = (h_fm[:, 4096 * (c % 2):4096 * (c % 2 + 1)]
                       .rearrange("p (k q) -> p k q", k=KD)[:, :, 128 * tw:128 * (tw + 1)])
                v.tensor_copy(dst, src.rearrange("p (k q) -> p k q", k=KD)
                              ).then_inc(s["h2fe"].h, 1)
                rec(("h2fe", t), s["h2fe"].inc(1))
            on("vector", h2fm_evac)

        def ffn_mm(te, c=c):
            if c == 0:
                te.wait_ge(s["sgn"].h, tok[("sgn", KD - 1)])
            for tw in range(TPC):
                t = 4 * c + tw
                bank = pwA if tw % 2 == 0 else pwB
                te.wait_ge(s["h2fe"].h, tok[("h2fe", t)])
                if tw < 2:
                    te.wait_ge(s["r1"].h, tok[("r1", t + 2)])
                else:
                    te.wait_ge(s["r2"].h, tok[("r2", t - 2)])
                for j in range(2):
                    for k in range(KD):
                        ins = te.matmul(bank[:, 512 * j:512 * (j + 1)],
                                        h_fm[:, 4096 * (c % 2) + 512 * k + 128 * tw:
                                             4096 * (c % 2) + 512 * k + 128 * (tw + 1)],
                                        wslot_b[:, 1024 * k + 512 * j:
                                                1024 * k + 512 * (j + 1)],
                                        start=(k == 0), stop=(k == KD - 1))
                ins.then_inc(s["ffp"].h, 1)
                rec(("ffp", t), s["ffp"].inc(1))
        on("tensor", ffn_mm)

        for tw in range(TPC):
            t = 4 * c + tw

            def resid2(v, t=t, tw=tw):
                bank = pwA if tw % 2 == 0 else pwB
                v.wait_ge(s["ffp"].h, tok[("ffp", t)])
                v.wait_ge(s["rsd2"].h, tok[("rsd2", t // 2)])
                col = 2 * ((t // 2) % 2) + (t % 2)
                xs = x_sb[:, 1024 * t:1024 * (t + 1)]
                v.scalar_tensor_tensor(xs, bank[:], rstd[:, col:col + 1], xs,
                                       ALU.mult, ALU.add).then_inc(s["r2"].h, 1)
                rec(("r2", t), s["r2"].inc(1))
            on("vector", resid2)

            def store(eng, t=t):
                eng.wait_ge(s["r2"].h, tok[("r2", t)])
                eng.dma_start(out=y_out[128 * t:128 * (t + 1), :],
                              in_=x_sb[:, 1024 * t:1024 * (t + 1)]
                              ).then_inc(s["st%d" % (t % 2)].h, 16)
                rec(("st", t), s["st%d" % (t % 2)].inc(16))
            on("sync", store)

    def final_wait(eng):
        eng.wait_ge(s["st0"].h, tok[("st", NT - 2)])
        eng.wait_ge(s["st1"].h, tok[("st", NT - 1)])
    on("sync", final_wait)

    es.close()
    return nc


_NC_CACHE = None


def kernel(L0_attn_x, L0_attn_W_emb, L0_attn_W_anchor, L0_attn_W_out,
           L0_attn_b_out, L0_ffn_weight):
    global _NC_CACHE
    x = np.ascontiguousarray(np.asarray(L0_attn_x, dtype=np.float32))
    W_emb = np.asarray(L0_attn_W_emb, dtype=np.float32)
    W_anchor = np.asarray(L0_attn_W_anchor, dtype=np.float32)
    W_out = np.asarray(L0_attn_W_out, dtype=np.float32)
    b_out = np.asarray(L0_attn_b_out, dtype=np.float32)
    W_ffn = np.asarray(L0_ffn_weight, dtype=np.float32)

    wqkT = np.ascontiguousarray(W_emb[:1024].T)
    wvT = np.ascontiguousarray(W_emb[1024:].T)
    wanchorT = np.ascontiguousarray(W_anchor.T)
    woutT = np.ascontiguousarray(W_out.T)
    wffnT = np.ascontiguousarray(W_ffn.T)
    boutr = np.ascontiguousarray(b_out.reshape(1, 1024))
    ident = np.eye(128, dtype=np.float32)

    if _NC_CACHE is None:
        _NC_CACHE = build_nc()
    nc = _NC_CACHE

    in_maps = []
    for c in range(8):
        b, half = c // 2, c % 2
        xs = np.ascontiguousarray(x[b, half * T:(half + 1) * T, :])
        in_maps.append({"x": xs, "wqkT": wqkT, "wvT": wvT, "woutT": woutT,
                        "wffnT": wffnT, "wanchorT": wanchorT, "bout": boutr,
                        "identin": ident})
    res = run_bass_kernel_spmd(nc, in_maps, core_ids=list(range(8)))
    out = np.empty((4, 4096, 1024), dtype=np.float32)
    for c in range(8):
        b, half = c // 2, c % 2
        out[b, half * T:(half + 1) * T, :] = res.results[c]["y"]
    return out


# version 31 (final)
# speedup vs baseline: 1.0833x; 1.0110x over previous
import numpy as np
import concourse.bass as bass
import concourse.mybir as mybir
from concourse.bass_utils import run_bass_kernel_spmd
from contextlib import ExitStack

F32 = mybir.dt.float32
BF16 = mybir.dt.bfloat16
AF = mybir.ActivationFunctionType
ALU = mybir.AluOpType
AX = mybir.AxisListType

T = 2048
NT = 16
NC = 4
TPC = 4
D = 1024
KD = 8
NP = 8
EPS = 1e-5
SCALE = 0.125


class Sem:
    def __init__(self, nc, es, name):
        self.h = es.enter_context(nc.semaphore(name))
        self.n = 0

    def inc(self, k=1):
        self.n += k
        return self.n


def build_nc():
    nc = bass.Bass("TRN2", target_bir_lowering=False)

    x_in = nc.dram_tensor("x", [T, D], F32, kind="ExternalInput").ap()
    wqkT = nc.dram_tensor("wqkT", [D, D], F32, kind="ExternalInput").ap()
    wvT = nc.dram_tensor("wvT", [D, D], F32, kind="ExternalInput").ap()
    woutT = nc.dram_tensor("woutT", [D, D], F32, kind="ExternalInput").ap()
    wffnT = nc.dram_tensor("wffnT", [D, D], F32, kind="ExternalInput").ap()
    wanchorT = nc.dram_tensor("wanchorT", [64, 64], F32, kind="ExternalInput").ap()
    bout = nc.dram_tensor("bout", [1, D], F32, kind="ExternalInput").ap()
    ident_in = nc.dram_tensor("identin", [128, 128], F32, kind="ExternalInput").ap()
    y_out = nc.dram_tensor("y", [T, D], F32, kind="ExternalOutput").ap()
    cc_in = nc.dram_tensor("cc_in", [128, 520], F32, kind="Internal")
    cc_out = nc.dram_tensor("cc_out", [128, 520], F32, kind="Internal")
    cc2_in = nc.dram_tensor("cc2_in", [128, 520], F32, kind="Internal")
    cc2_out = nc.dram_tensor("cc2_out", [128, 520], F32, kind="Internal")
    ccw_in = nc.dram_tensor("ccw_in", [128, 8], F32, kind="Internal")
    ccw_out = nc.dram_tensor("ccw_out", [128, 8], F32, kind="Internal")

    es = ExitStack()
    sb = lambda n, s_, d: es.enter_context(nc.sbuf_tensor(n, s_, d))
    psum = lambda n, s_, d: es.enter_context(nc.psum_tensor(n, s_, d))

    x_sb = sb("x_sb", [128, NT * 1024], F32)
    h_tm = sb("h_tm", [128, 4 * 1024], BF16)
    stats = sb("stats", [128, 8 * 12], F32)
    mv = sb("mv", [128, 8 * 2], F32)
    veps = sb("veps", [128, 8], F32)
    lnv = sb("lnv", [128, 8], F32)
    negm = sb("negm", [128, 8], F32)
    rstd = sb("rstd", [128, 8], F32)
    nmr = sb("nmr", [128, 8], F32)
    h_fm = sb("h_fm", [128, 2 * 4096], BF16)
    wslot_a = sb("wslot_a", [128, 8192], BF16)
    wslot_b = sb("wslot_b", [128, 8192], BF16)
    qk_sb = sb("qk_sb", [128, 2 * 4096], BF16)
    e_am = sb("e_am", [128, 8 * 512], BF16)
    eh_tm = sb("eh_tm", [128, NT * 1024], BF16)
    rs_sb = sb("rs_sb", [128, 64], F32)
    recip_sb = sb("recip_sb", [128, 64], F32)
    v_aug = sb("v_aug", [128, 8 * 1040], BF16)
    agg_sb = sb("agg_sb", [128, 1040], F32)
    agg_sb2 = h_tm[:, 0:2080].bitcast(F32)
    av_blk = sb("av_blk", [128, 1024], BF16)
    anchor_blk = sb("anchor_blk", [128, 128], BF16)
    ident_sb = sb("ident_sb", [128, 128], BF16)
    bias_sb = sb("bias_sb", [1, 1024], BF16)
    ones_row = sb("ones_row", [1, 128], BF16)
    wffn_stage = sb("wffn_stage", [128, 1024], F32)
    final_sb = v_aug
    sguard = sb("sguard", [128, 32], F32)

    pT = psum("pT", [128, 2048], BF16)
    pQ = psum("pQ", [128, 1024], F32)
    pwA = psum("pwA", [128, 1024], F32)
    pwB = psum("pwB", [128, 1024], F32)

    blk = es.enter_context(nc.Block())

    def on(eng, f):
        getattr(blk, eng)(f)

    s = {}
    for name in (["wv%d" % k for k in range(8)] + ["wqk%d" % k for k in range(8)]
                 + ["ve0", "rsd0", "x0", "x0b", "x1", "x2", "x3", "wsm", "wqk", "wv", "wout",
                 "wf0", "wf1", "st0", "st1", "init", "ve", "h", "ht",
                 "hfe", "qkp", "qke", "vp", "vev", "scp", "ee", "etb",
                 "norm", "agp", "age", "sgn", "ccin", "cc",
                 "ccout", "av", "eat", "eae", "bcp", "bce", "wop", "r1", "rsd", "rsd2", "algn",
                 "ve2", "h2", "h2t", "h2fe", "ffp", "r2"]):
        s[name] = Sem(nc, es, "s_" + name)

    tok = {}
    for c in range(NC):
        tok[("x", c)] = 16
    tok["init"] = 1
    tok["ve0"] = 1
    tok["rsd0"] = 1
    tok["wsm"] = 16 * 4
    tok["wqk"] = 16 * KD
    tok["wv"] = 16 * KD
    tok["wout"] = 16 * KD
    for t in range(NT):
        for nm_ in ["h", "ht", "hfe", "etb", "norm",
                    "wop", "r1", "ve2", "h2", "h2t", "h2fe",
                    "ffp", "r2"]:
            tok[(nm_, t)] = t + 1
        tok[("st", t)] = 16 * (t // 2 + 1)

    for m in range(NC):
        for p in range(NP):
            for tw in range(TPC):
                tok[("agp", 4 * m + tw, p)] = 32 * m + 4 * p + tw + 1
            tok[("age", m, p)] = 8 * m + p + 1
    for c in range(NC):
        tok[("ve", c)] = c + 1
        tok[("rsd", c)] = c + 1
    for pi in range(NT // 2):
        tok[("ve2p", pi)] = pi + 1
        tok[("rsd2", pi)] = pi + 1
    for c in range(NC):
        for i in range(8):
            for nm_ in ["qkp", "qke", "vp", "vev", "scp", "ee",
                        "eat", "eae", "bcp", "bce"]:
                tok[(nm_, c, i)] = c * 8 + i + 1
        tok[("qklast", c)] = tok[("qkp", c, KD - 1)]
        tok[("vlast", c)] = tok[("vp", c, 7)]
        tok[("vevlast", c)] = tok[("vev", c, 7)]
    for j in range(KD):
        tok[("sgn", j)] = j + 1
        tok[("wf", j)] = 16 * (j + 1)
    tok["ccin"] = 48
    tok["cc"] = 1
    tok["ccout"] = 32
    for p in range(NP):
        tok[("av", p)] = p + 1

    def rec(key, val):
        assert tok.get(key, val) == val, (key, tok.get(key), val)
        tok[key] = val

    def init_sync(eng):
        eng.dma_start(
            out=x_sb[:, 0:2048].rearrange("p (a d) -> p a d", a=2),
            in_=x_in[0:256, :].rearrange("(a p) d -> p a d", p=128),
        ).then_inc(s["x0"].h, 16)
        rec(("x", 0), s["x0"].inc(16))
        eng.dma_start(
            out=x_sb[:, 2048:4096].rearrange("p (a d) -> p a d", a=2),
            in_=x_in[256:512, :].rearrange("(a p) d -> p a d", p=128),
        ).then_inc(s["x0b"].h, 16)
        rec(("x0b", 0), s["x0b"].inc(16))
        for c in range(1, NC):
            eng.dma_start(
                out=x_sb[:, 4096 * c:4096 * (c + 1)].rearrange("p (a d) -> p a d", a=4),
                in_=x_in[512 * c:512 * (c + 1), :].rearrange("(a p) d -> p a d", p=128),
            ).then_inc(s["x%d" % c].h, 16)
            rec(("x", c), s["x%d" % c].inc(16))
    on("sync", init_sync)

    def init_gp(eng):
        eng.memset(ones_row[:], 1.0)
        eng.memset(v_aug[:], 1.0)
        eng.memset(av_blk[:], 0.0)
        eng.memset(anchor_blk[:], 0.0)
        eng.drain()
        eng.nop().then_inc(s["init"].h, 1)
        rec("init", s["init"].inc(1))
        eng.dma_start(out=anchor_blk[0:64, 0:64], in_=wanchorT).then_inc(s["wsm"].h, 16)
        s["wsm"].inc(16)
        eng.dma_start(out=anchor_blk[64:128, 64:128], in_=wanchorT).then_inc(s["wsm"].h, 16)
        s["wsm"].inc(16)
        eng.dma_start(out=ident_sb[:], in_=ident_in).then_inc(s["wsm"].h, 16)
        s["wsm"].inc(16)
        eng.dma_start(out=bias_sb[:], in_=bout).then_inc(s["wsm"].h, 16)
        rec("wsm", s["wsm"].inc(16))
        for k in range(KD):
            eng.dma_start(out=wslot_b[:, 1024 * k:1024 * (k + 1)],
                          in_=wvT[128 * k:128 * (k + 1), :]).then_inc(s["wv%d" % k].h, 16)
            s["wv%d" % k].inc(16)
        for k in range(KD):
            eng.dma_start(out=wslot_a[:, 1024 * k:1024 * (k + 1)],
                          in_=wqkT[128 * k:128 * (k + 1), :]).then_inc(s["wqk%d" % k].h, 16)
            s["wqk%d" % k].inc(16)
        eng.memset(sguard[:, 24:32], 0.0)
        eng.drain()
        eng.dma_start(out=ccw_in.ap(), in_=sguard[:, 24:32]).then_inc(s["ccin"].h, 16)
        s["ccin"].inc(16)
        eng.wait_ge(s["ccin"].h, 16)
        eng.collective_compute("AllReduce", ALU.add,
                               [[0, 1], [2, 3], [4, 5], [6, 7]],
                               ins=[ccw_in.ap()], outs=[ccw_out.ap()]
                               ).then_inc(s["init"].h, 1)
        s["init"].inc(1)
        eng.collective_compute("AllReduce", ALU.add,
                               [[0, 1], [2, 3], [4, 5], [6, 7]],
                               ins=[ccw_in.ap()], outs=[ccw_out.ap()]
                               ).then_inc(s["algn"].h, 1)
        s["algn"].inc(1)
    on("gpsimd", init_gp)


    def emit_A_statsA(c, scalar_too=True):
        so = 4 * (c % 2)

        def ln_stats(v, c=c, so=so):
            if c == 3:
                v.wait_ge(s["algn"].h, 1)
            v.wait_ge(s["x%d" % c].h, tok[("x", c)])
            if c >= 2:
                v.wait_ge(s["h"].h, tok[("h", 4 * (c - 2) + 3)])
            tws = list(range(TPC))
            if c == 0:
                xs = x_sb[:, 0:1024]
                v.bn_stats(stats[:, 12 * so:12 * so + 6], xs[:, 0:512])
                v.bn_stats(stats[:, 12 * so + 6:12 * so + 12], xs[:, 512:1024])
                v.drain()
                v.bn_aggr(mv[:, 2 * so:2 * so + 2], stats[:, 12 * so:12 * so + 12])
                v.drain()
                v.tensor_scalar(veps[:, so:so + 1], mv[:, 2 * so + 1:2 * so + 2],
                                EPS, None, ALU.add)
                v.tensor_scalar(negm[:, so:so + 1], mv[:, 2 * so:2 * so + 1],
                                -1.0, None, ALU.mult).then_inc(s["ve0"].h, 1)
                rec("ve0", s["ve0"].inc(1))
                tws = [1, 2, 3]
            for tw in tws:
                if c == 0 and tw == 2:
                    v.wait_ge(s["x0b"].h, tok[("x0b", 0)])
                t = 4 * c + tw
                xs = x_sb[:, 1024 * t:1024 * (t + 1)]
                v.bn_stats(stats[:, 12 * (so + tw):12 * (so + tw) + 6], xs[:, 0:512])
                v.bn_stats(stats[:, 12 * (so + tw) + 6:12 * (so + tw) + 12],
                           xs[:, 512:1024])
            v.drain()
            for tw in tws:
                v.bn_aggr(mv[:, 2 * (so + tw):2 * (so + tw) + 2],
                          stats[:, 12 * (so + tw):12 * (so + tw) + 12])
            v.drain()
            lo = tws[0]
            mvv = mv[:, 2 * (so + lo):2 * so + 8].rearrange("p (a b) -> p a b", b=2)
            v.tensor_scalar(veps[:, so + lo:so + 4], mvv[:, :, 1:2].squeeze(2),
                            EPS, None, ALU.add)
            v.tensor_scalar(negm[:, so + lo:so + 4], mvv[:, :, 0:1].squeeze(2),
                            -1.0, None, ALU.mult).then_inc(s["ve"].h, 1)
            rec(("ve", c), s["ve"].inc(1))
        on("vector", ln_stats)

        def ln_scalar(sc, c=c, so=so):
            if c == 3:
                sc.wait_ge(s["algn"].h, 1)
            lo = 0
            if c == 0:
                sc.wait_ge(s["ve0"].h, tok["ve0"])
                sc.activation(lnv[:, so:so + 1], veps[:, so:so + 1], AF.Ln)
                sc.drain()
                sc.activation(rstd[:, so:so + 1], lnv[:, so:so + 1], AF.Exp,
                              scale=-0.5).then_inc(s["rsd0"].h, 1)
                rec("rsd0", s["rsd0"].inc(1))
                lo = 1
            sc.wait_ge(s["ve"].h, tok[("ve", c)])
            sc.activation(lnv[:, so + lo:so + 4], veps[:, so + lo:so + 4], AF.Ln)
            sc.drain()
            sc.activation(rstd[:, so + lo:so + 4], lnv[:, so + lo:so + 4], AF.Exp,
                          scale=-0.5).then_inc(s["rsd"].h, 1)
            rec(("rsd", c), s["rsd"].inc(1))
        if scalar_too:
            on("scalar", ln_scalar)
        else:
            emit_A_statsA.deferred = ln_scalar

    def emit_A_statsB(c):
        so = 4 * (c % 2)

        def ln_apply(v, c=c, so=so):
            lo = 0
            if c == 0:
                v.wait_ge(s["rsd0"].h, tok["rsd0"])
                v.scalar_tensor_tensor(nmr[:, so:so + 1], negm[:, so:so + 1], 1.0,
                                       rstd[:, so:so + 1], ALU.mult, ALU.mult)
                v.drain()
                v.tensor_scalar(h_tm[:, 0:1024], x_sb[:, 0:1024],
                                rstd[:, so:so + 1], nmr[:, so:so + 1],
                                ALU.mult, ALU.add).then_inc(s["h"].h, 1)
                rec(("h", 0), s["h"].inc(1))
                lo = 1
            v.wait_ge(s["rsd"].h, tok[("rsd", c)])
            v.scalar_tensor_tensor(nmr[:, so + lo:so + 4], negm[:, so + lo:so + 4],
                                   1.0, rstd[:, so + lo:so + 4],
                                   ALU.mult, ALU.mult)
            v.drain()
            for tw in range(lo, TPC):
                t = 4 * c + tw
                if c >= 1 and tw == 0:
                    v.wait_ge(s["ht"].h, tok[("ht", 4 * (c - 1) + 3)])
                v.tensor_scalar(h_tm[:, 1024 * tw:1024 * (tw + 1)],
                                x_sb[:, 1024 * t:1024 * (t + 1)],
                                rstd[:, so + tw:so + tw + 1],
                                nmr[:, so + tw:so + tw + 1],
                                ALU.mult, ALU.add).then_inc(s["h"].h, 1)
                rec(("h", t), s["h"].inc(1))
        on("vector", ln_apply)

    def emit_A1(c):
        for tw in range(TPC):
            t = 4 * c + tw
            sl = t % 2

            def h_transpose(te, t=t, sl=sl, c=c, tw=tw):
                if c == 3 and tw == 0:
                    te.wait_ge(s["algn"].h, 1)
                te.wait_ge(s["h"].h, tok[("h", t)])
                te.wait_ge(s["wsm"].h, tok["wsm"])
                if t >= 2:
                    te.wait_ge(s["hfe"].h, tok[("hfe", t - 2)])
                dst = pT[:, 1024 * sl:1024 * (sl + 1)]
                for k in range(KD):
                    ins = te.matmul(dst[:, 128 * k:128 * (k + 1)],
                                    h_tm[:, 1024 * tw + 128 * k:1024 * tw + 128 * (k + 1)],
                                    ident_sb[:], start=(k == 0), stop=(k == KD - 1),
                                    is_transpose=True)
                ins.then_inc(s["ht"].h, 1)
                rec(("ht", t), s["ht"].inc(1))
            on("tensor", h_transpose)

            def hfm_evac(v, t=t, tw=tw, c=c, sl=sl):
                v.wait_ge(s["ht"].h, tok[("ht", t)])
                if c >= 2:
                    v.wait_ge(s["vp"].h, tok[("vlast", c - 2)])
                    v.wait_ge(s["qkp"].h, tok[("qklast", c - 2)])
                src = pT[:, 1024 * sl:1024 * (sl + 1)]
                dst = (h_fm[:, 4096 * (c % 2):4096 * (c % 2 + 1)]
                       .rearrange("p (k q) -> p k q", k=KD)[:, :, 128 * tw:128 * (tw + 1)])
                v.tensor_copy(dst, src.rearrange("p (k q) -> p k q", k=KD)
                              ).then_inc(s["hfe"].h, 1)
                rec(("hfe", t), s["hfe"].inc(1))
            on("vector", hfm_evac)

    def emit_A2(c):
        def emb_v(te, c=c):
            for tw in range(TPC):
                te.wait_ge(s["hfe"].h, tok[("hfe", 4 * c + tw)])
                for vo in range(2):
                    g = 2 * tw + vo
                    half = g % 2
                    seq = c * 8 + g
                    if seq >= 2:
                        gg = seq - 2
                        te.wait_ge(s["vev"].h, tok[("vev", gg // 8, gg % 8)])
                    if c >= 2 and g < 2:
                        te.wait_ge(s["norm"].h, tok[("norm", 4 * (c - 2) + g)])
                    for k in range(KD):
                        if c == 0 and tw == 0:
                            te.wait_ge(s["wv%d" % k].h, 16)
                        ins = te.matmul(pwA[:, 512 * half:512 * (half + 1)],
                                        h_fm[:, 4096 * (c % 2) + 512 * k + 128 * tw:
                                             4096 * (c % 2) + 512 * k + 128 * (tw + 1)],
                                        wslot_b[:, 1024 * k + 512 * vo:
                                                1024 * k + 512 * (vo + 1)],
                                        start=(k == 0), stop=(k == KD - 1))
                    ins.then_inc(s["vp"].h, 1)
                    rec(("vp", c, g), s["vp"].inc(1))
        on("tensor", emb_v)

        def v_evac(v, c=c):
            for tw in range(TPC):
                t = 4 * c + tw
                tsl = t % 8
                for vo in range(2):
                    g = 2 * tw + vo
                    half = g % 2
                    v.wait_ge(s["vp"].h, tok[("vp", c, g)])
                    if t >= 8:
                        v.wait_ge(s["agp"].h, tok[("agp", t - 8, NP - 1)])
                    if t == 0 and vo == 0:
                        v.wait_ge(s["init"].h, tok["init"])
                    src = pwA[:, 512 * half:512 * (half + 1)].rearrange(
                        "p (q h d) -> p q h d", q=4, h=2)
                    base = v_aug[:, 1040 * tsl + 520 * vo:1040 * tsl + 520 * (vo + 1)]
                    dst = (base.rearrange("p (q r) -> p q r", q=4)
                           .rearrange("p q (h e) -> p q h e", h=2)[:, :, :, 0:64])
                    v.tensor_copy(dst, src).then_inc(s["vev"].h, 1)
                    rec(("vev", c, g), s["vev"].inc(1))
        on("vector", v_evac)

        def emb_qk(te, c=c):
            for o in range(KD):
                half = o % 2
                seq = c * KD + o
                if seq >= 2:
                    oo = seq - 2
                    te.wait_ge(s["qke"].h, tok[("qke", oo // KD, oo % KD)])
                if c >= 2 and o < 2:
                    te.wait_ge(s["age"].h, tok[("age", c - 2, 6 + o)])
                for k in range(KD):
                    if c == 0 and o == 0:
                        te.wait_ge(s["wqk%d" % k].h, 16)
                    ins = te.matmul(pQ[:, 512 * half:512 * (half + 1)],
                                    wslot_a[:, 1024 * k + 128 * o:1024 * k + 128 * (o + 1)],
                                    h_fm[:, 4096 * (c % 2) + 512 * k:
                                         4096 * (c % 2) + 512 * (k + 1)],
                                    start=(k == 0), stop=(k == KD - 1))
                ins.then_inc(s["qkp"].h, 1)
                rec(("qkp", c, o), s["qkp"].inc(1))
        on("tensor", emb_qk)

        def qk_evac(v, c=c):
            for o in range(KD):
                half = o % 2
                v.wait_ge(s["qkp"].h, tok[("qkp", c, o)])
                if c >= 2:
                    v.wait_ge(s["scp"].h, tok[("scp", c - 2, o)])
                v.tensor_copy(qk_sb[:, 4096 * (c % 2) + 512 * o:
                                    4096 * (c % 2) + 512 * (o + 1)],
                              pQ[:, 512 * half:512 * (half + 1)]).then_inc(s["qke"].h, 1)
                rec(("qke", c, o), s["qke"].inc(1))
        on("vector", qk_evac)

    def emit_M(m):
        et_banks = [pwA[:, 0:512].bitcast(BF16), pwA[:, 512:1024].bitcast(BF16),
                    pwB[:, 0:512].bitcast(BF16), pwB[:, 512:1024].bitcast(BF16)]

        def te_middle(te, m=m):
            for p in range(NP):
                half = p % 2
                if p < 2:
                    if m < NC - 1:
                        te.wait_ge(s["qke"].h, tok[("qke", m + 1, 6 + p)])
                    else:
                        te.wait_ge(s["age"].h, tok[("age", m - 1, 6 + p)])
                else:
                    te.wait_ge(s["ee"].h, tok[("ee", m, p - 2)])
                te.wait_ge(s["qke"].h, tok[("qke", m, p)])
                te.matmul(pQ[:, 512 * half:512 * (half + 1)], anchor_blk[:],
                          qk_sb[:, 4096 * (m % 2) + 512 * p:4096 * (m % 2) + 512 * (p + 1)],
                          start=True, stop=True).then_inc(s["scp"].h, 1)
                rec(("scp", m, p), s["scp"].inc(1))
            for tw in range(TPC):
                t = 4 * m + tw
                if tw < 2:
                    if m < NC - 1:
                        te.wait_ge(s["vev"].h, tok[("vev", m + 1, 6 + tw)])
                    else:
                        te.wait_ge(s["norm"].h, tok[("norm", 4 * (m - 1) + tw)])
                else:
                    if m >= 1:
                        te.wait_ge(s["norm"].h, tok[("norm", 4 * (m - 1) + tw)])
                for p in range(NP):
                    te.wait_ge(s["ee"].h, tok[("ee", m, p)])
                    ins = te.matmul(et_banks[tw][:, 128 * p:128 * (p + 1)],
                                    e_am[:, 512 * p + 128 * tw:512 * p + 128 * (tw + 1)],
                                    ident_sb[:], start=(p == 0), stop=(p == NP - 1),
                                    is_transpose=True)
                ins.then_inc(s["etb"].h, 1)
                rec(("etb", t), s["etb"].inc(1))
            for p in range(NP):
                half = p % 2
                if p < 2:
                    te.wait_ge(s["ee"].h, tok[("ee", m, 6 + p)])
                else:
                    te.wait_ge(s["age"].h, tok[("age", m, p - 2)])
                for tw in range(TPC):
                    t = 4 * m + tw
                    tsl = t % 8
                    if p == 0:
                        te.wait_ge(s["norm"].h, tok[("norm", t)])
                    te.matmul(pQ[:, 512 * half:512 * half + 130],
                              eh_tm[:, 1024 * t + 128 * p:1024 * t + 128 * (p + 1)],
                              v_aug[:, 1040 * tsl + 130 * p:1040 * tsl + 130 * (p + 1)],
                              start=(tw == 0), stop=(tw == TPC - 1)
                              ).then_inc(s["agp"].h, 1)
                    rec(("agp", t, p), s["agp"].inc(1))
        on("tensor", te_middle)

        def exp_evac(sc, m=m):
            for p in range(NP):
                half = p % 2
                sc.wait_ge(s["scp"].h, tok[("scp", m, p)])
                if m >= 1:
                    sc.wait_ge(s["etb"].h, tok[("etb", 4 * (m - 1) + 3)])
                sc.activation(e_am[:, 512 * p:512 * (p + 1)],
                              pQ[:, 512 * half:512 * (half + 1)],
                              AF.Exp, scale=SCALE).then_inc(s["ee"].h, 1)
                rec(("ee", m, p), s["ee"].inc(1))
        on("scalar", exp_evac)

        def dve_middle(v, m=m):
            def emit_norm_pair(tw0):
                for tw in (tw0, tw0 + 1):
                    t = 4 * m + tw
                    v.wait_ge(s["etb"].h, tok[("etb", t)])
                    src4 = et_banks[tw].rearrange("p (g h a) -> p g h a", g=NP, h=2)
                    v.tensor_reduce(rs_sb[:, 16 * tw:16 * (tw + 1)]
                                    .rearrange("p (g h) -> p g h", g=NP).unsqueeze(3),
                                    src4, AX.X, ALU.add)
                v.drain()
                v.reciprocal(recip_sb[:, 16 * tw0:16 * (tw0 + 2)],
                             rs_sb[:, 16 * tw0:16 * (tw0 + 2)])
                v.drain()
                for tw in (tw0, tw0 + 1):
                    t = 4 * m + tw
                    src4 = et_banks[tw].rearrange("p (g h a) -> p g h a", g=NP, h=2)
                    rb = (recip_sb[:, 16 * tw:16 * (tw + 1)]
                          .rearrange("p (g h) -> p g h", g=NP).unsqueeze(3)
                          .broadcast_to([128, NP, 2, 64]))
                    v.tensor_tensor(eh_tm[:, 1024 * t:1024 * (t + 1)]
                                    .rearrange("p (g h a) -> p g h a", g=NP, h=2),
                                    src4, rb, ALU.mult).then_inc(s["norm"].h, 1)
                    rec(("norm", t), s["norm"].inc(1))

            emit_norm_pair(0)
            emit_norm_pair(2)
            for p in range(NP):
                half = p % 2
                v.wait_ge(s["agp"].h, tok[("agp", 4 * m + 3, p)])
                src = pQ[:, 512 * half:512 * half + 130]
                buf = agg_sb if m < 2 else agg_sb2
                dst = buf[:, 130 * p:130 * (p + 1)]
                if m in (0, 2):
                    ins = v.tensor_copy(dst, src)
                else:
                    ins = v.tensor_tensor(dst, dst, src, ALU.add)
                ins.then_inc(s["age"].h, 1)
                rec(("age", m, p), s["age"].inc(1))
        on("vector", dve_middle)

    emit_A_statsA(0)
    emit_A_statsB(0)
    emit_A1(0)
    emit_A_statsA(1)
    emit_A2(0)
    emit_A_statsB(1)
    emit_A1(1)
    emit_A_statsA(2)
    emit_A2(1)
    emit_A_statsB(2)
    emit_A1(2)
    emit_A_statsA(3, scalar_too=False)
    emit_M(0)
    on("scalar", emit_A_statsA.deferred)
    emit_A2(2)
    emit_A_statsB(3)
    emit_A1(3)
    emit_M(1)
    emit_A2(3)
    emit_M(2)
    emit_M(3)

    def late_weights(eng):
        eng.wait_ge(s["qkp"].h, tok[("qklast", NC - 1)])
        for k in range(KD):
            eng.dma_start(out=wslot_a[:, 1024 * k:1024 * (k + 1)],
                          in_=woutT[128 * k:128 * (k + 1), :]).then_inc(s["wout"].h, 16)
            s["wout"].inc(16)
    on("gpsimd", late_weights)

    def wffn_loads(eng):
        for j in range(KD):
            if j >= 1:
                eng.wait_ge(s["sgn"].h, tok[("sgn", j - 1)])
            eng.dma_start(out=wffn_stage[:],
                          in_=wffnT[128 * j:128 * (j + 1), :]).then_inc(s["wf0"].h, 16)
            rec(("wf", j), s["wf0"].inc(16))
    on("sync", wffn_loads)

    def sign_ffn(sc):
        sc.wait_ge(s["vp"].h, tok[("vlast", NC - 1)])
        for j in range(KD):
            sc.wait_ge(s["wf0"].h, tok[("wf", j)])
            sc.sign(wslot_b[:, 1024 * j:1024 * (j + 1)],
                    wffn_stage[:]).then_inc(s["sgn"].h, 1)
            rec(("sgn", j), s["sgn"].inc(1))
    on("scalar", sign_ffn)

    def collective(eng):
        def compact(dst_dram, sb_src):
            top = sb_src[0:64, :].rearrange("p (g r) -> p g r", g=NP)[:, :, 0:65]
            bot = sb_src[64:128, :].rearrange("p (g r) -> p g r", g=NP)[:, :, 65:130]
            eng.dma_start(out=dst_dram.ap()[0:64, :].rearrange("p (g r) -> p g r", g=NP),
                          in_=top).then_inc(s["ccin"].h, 16)
            s["ccin"].inc(16)
            eng.dma_start(out=dst_dram.ap()[64:128, :].rearrange("p (g r) -> p g r", g=NP),
                          in_=bot).then_inc(s["ccin"].h, 16)
            s["ccin"].inc(16)

        eng.wait_ge(s["age"].h, tok[("age", 1, NP - 1)])
        compact(cc_in, agg_sb[:])
        eng.wait_ge(s["ccin"].h, s["ccin"].n)
        eng.collective_compute("AllReduce", ALU.add,
                               [[0, 1], [2, 3], [4, 5], [6, 7]],
                               ins=[cc_in.ap()], outs=[cc_out.ap()]
                               ).then_inc(s["cc"].h, 1)
        s["cc"].inc(1)
        eng.wait_ge(s["age"].h, tok[("age", NC - 1, NP - 1)])
        compact(cc2_in, agg_sb2[:, 0:1040])
        eng.wait_ge(s["ccin"].h, s["ccin"].n)
        eng.collective_compute("AllReduce", ALU.add,
                               [[0, 1], [2, 3], [4, 5], [6, 7]],
                               ins=[cc2_in.ap()], outs=[cc2_out.ap()]
                               ).then_inc(s["cc"].h, 1)
        s["cc"].inc(1)
        eng.wait_ge(s["cc"].h, s["cc"].n)
        eng.dma_start(out=agg_sb[:, 0:520], in_=cc_out.ap()).then_inc(s["ccout"].h, 16)
        s["ccout"].inc(16)
        eng.dma_start(out=agg_sb[:, 520:1040], in_=cc2_out.ap()).then_inc(s["ccout"].h, 16)
        rec("ccout", s["ccout"].inc(16))
    on("gpsimd", collective)

    def av_prep(v):
        v.wait_ge(s["ccout"].h, tok["ccout"])
        v.tensor_tensor(agg_sb[:, 0:520], agg_sb[:, 0:520], agg_sb[:, 520:1040],
                        ALU.add)
        v.drain()
        scol = agg_sb[:, 0:520].rearrange("p (g r) -> p g r", g=NP)[:, :, 64:65].squeeze(2)
        iszero = sguard[:, 0:8]
        ssafe = sguard[:, 8:16]
        rg = sguard[:, 24:32]
        v.tensor_single_scalar(iszero, scol, 0.0, ALU.is_equal)
        v.scalar_tensor_tensor(ssafe, scol, 0.0, scol, ALU.is_equal, ALU.add)
        v.drain()
        v.reciprocal(rg, ssafe)
        v.drain()
        v.tensor_tensor(rg, rg, iszero, ALU.subtract)
        v.drain()
        for p in range(NP):
            v.tensor_scalar(av_blk[0:64, 128 * p:128 * p + 64],
                            agg_sb[0:64, 65 * p:65 * p + 64],
                            rg[0:64, p:p + 1], None, ALU.mult)
            v.tensor_scalar(av_blk[64:128, 128 * p + 64:128 * p + 128],
                            agg_sb[64:128, 65 * p:65 * p + 64],
                            rg[64:128, p:p + 1], None,
                            ALU.mult).then_inc(s["av"].h, 1)
            rec(("av", p), s["av"].inc(1))
    c0_halves_bf = [pT[:, 0:1024], pT[:, 1024:2048]]
    c0_halves = [None, None, pwA[:, 0:512], pwA[:, 512:1024],
                 pwB[:, 0:512], pwB[:, 512:1024], pQ[:, 0:512], pQ[:, 512:1024]]

    def _st2_c0_evac_a(v):
        for p in range(NP):
            v.wait_ge(s["eat"].h, tok[("eat", 0, p)])
            srcp = (c0_halves_bf[p] if p < 2
                    else c0_halves[p].bitcast(BF16))[:, 0:512]
            v.tensor_copy(qk_sb[:, 512 * p:512 * (p + 1)],
                          srcp).then_inc(s["eae"].h, 1)
            rec(("eae", 0, p), s["eae"].inc(1))
    on("vector", _st2_c0_evac_a)
    on("vector", av_prep)

    def _st2_c0(te):
        for p in range(NP):
            if p < 2:
                te.wait_ge(s["hfe"].h, tok[("hfe", NT - 1)])
            elif p < 6:
                te.wait_ge(s["norm"].h, tok[("norm", 12 + (p - 2))])
            else:
                te.wait_ge(s["age"].h, tok[("age", NC - 1, p)])
            dst = c0_halves_bf[p] if p < 2 else c0_halves[p].bitcast(BF16)
            for tw in range(TPC):
                ins = te.matmul(dst[:, 128 * tw:128 * (tw + 1)],
                                eh_tm[:, 1024 * tw + 128 * p:1024 * tw + 128 * (p + 1)],
                                ident_sb[:], start=(tw == 0), stop=(tw == TPC - 1),
                                is_transpose=True)
            ins.then_inc(s["eat"].h, 1)
            rec(("eat", 0, p), s["eat"].inc(1))
        for p in range(NP):
            half = p % 2
            te.wait_ge(s["av"].h, tok[("av", p)])
            if p < 2:
                te.wait_ge(s["eae"].h, tok[("eae", 0, 6 + p)])
            else:
                te.wait_ge(s["bce"].h, tok[("bce", 0, p - 2)])
            te.wait_ge(s["eae"].h, tok[("eae", 0, p)])
            te.matmul(pQ[:, 512 * half:512 * (half + 1)],
                      av_blk[:, 128 * p:128 * (p + 1)],
                      qk_sb[:, 512 * p:512 * (p + 1)],
                      start=True, stop=True).then_inc(s["bcp"].h, 1)
            rec(("bcp", 0, p), s["bcp"].inc(1))
    def _st2_c0_evac(v):
        for p in range(NP):
            half = p % 2
            v.wait_ge(s["bcp"].h, tok[("bcp", 0, p)])
            v.tensor_copy(final_sb[:, 512 * p:512 * (p + 1)],
                          pQ[:, 512 * half:512 * (half + 1)]).then_inc(s["bce"].h, 1)
            rec(("bce", 0, p), s["bce"].inc(1))
    def _st2_pq(te, c):
        for p in range(NP):
            half = p % 2
            if p < 2:
                te.wait_ge(s["bce"].h, tok[("bce", c - 1, 6 + p)])
            else:
                te.wait_ge(s["bce"].h, tok[("bce", c, p - 2)])
            dst = pQ[:, 512 * half:512 * (half + 1)].bitcast(BF16)
            for tw in range(TPC):
                t = 4 * c + tw
                ins = te.matmul(dst[:, 128 * tw:128 * (tw + 1)],
                                eh_tm[:, 1024 * t + 128 * p:1024 * t + 128 * (p + 1)],
                                ident_sb[:], start=(tw == 0), stop=(tw == TPC - 1),
                                is_transpose=True)
            ins.then_inc(s["eat"].h, 1)
            rec(("eat", c, p), s["eat"].inc(1))
            te.wait_ge(s["eae"].h, tok[("eae", c, p)])
            te.matmul(pQ[:, 512 * half:512 * (half + 1)],
                      av_blk[:, 128 * p:128 * (p + 1)],
                      qk_sb[:, 4096 * (c % 2) + 512 * p:
                           4096 * (c % 2) + 512 * (p + 1)],
                      start=True, stop=True).then_inc(s["bcp"].h, 1)
            rec(("bcp", c, p), s["bcp"].inc(1))

    def _st2_pq_evac(sc, c, plist):
        for p in plist:
            half = p % 2
            sc.wait_ge(s["eat"].h, tok[("eat", c, p)])
            if c >= 2:
                sc.wait_ge(s["bcp"].h, tok[("bcp", c - 2, p)])
            sc.copy(qk_sb[:, 4096 * (c % 2) + 512 * p:
                          4096 * (c % 2) + 512 * (p + 1)],
                    pQ[:, 512 * half:512 * (half + 1)].bitcast(BF16)[:, 0:512]
                    ).then_inc(s["eae"].h, 1)
            rec(("eae", c, p), s["eae"].inc(1))
            sc.wait_ge(s["bcp"].h, tok[("bcp", c, p)])
            if p == 0 and c >= 2:
                sc.wait_ge(s["wop"].h, tok[("wop", 4 * (c - 2) + 3)])
            sc.copy(final_sb[:, 4096 * (c % 2) + 512 * p:
                             4096 * (c % 2) + 512 * (p + 1)],
                    pQ[:, 512 * half:512 * (half + 1)]).then_inc(s["bce"].h, 1)
            rec(("bce", c, p), s["bce"].inc(1))

    on("tensor", _st2_c0)
    on("vector", _st2_c0_evac)

    for c in range(NC):
        def wout_mm(te, c=c):
            for tw in range(TPC):
                t = 4 * c + tw
                bank = pwA if tw % 2 == 0 else pwB
                if tw >= 2:
                    te.wait_ge(s["r1"].h, tok[("r1", t - 2)])
                elif c >= 1:
                    te.wait_ge(s["r2"].h, tok[("r2", t - 2)])
                elif tw == 0:
                    te.wait_ge(s["eae"].h, tok[("eae", 0, 5)])
                    te.wait_ge(s["init"].h, tok["init"])
                    te.wait_ge(s["wsm"].h, tok["wsm"])
                    te.wait_ge(s["wout"].h, tok["wout"])
                for j in range(2):
                    te.matmul(bank[:, 512 * j:512 * (j + 1)], ones_row[:],
                              bias_sb[:, 512 * j:512 * (j + 1)], start=True, stop=False)
                for k in range(KD):
                    if tw == 0:
                        te.wait_ge(s["bce"].h, tok[("bce", c, k)])
                    for j in range(2):
                        ins = te.matmul(bank[:, 512 * j:512 * (j + 1)],
                                        final_sb[:, 4096 * (c % 2) + 512 * k + 128 * tw:
                                                4096 * (c % 2) + 512 * k + 128 * (tw + 1)],
                                        wslot_a[:, 1024 * k + 512 * j:
                                                1024 * k + 512 * (j + 1)],
                                        start=False, stop=(k == KD - 1))
                ins.then_inc(s["wop"].h, 1)
                rec(("wop", t), s["wop"].inc(1))
        on("tensor", wout_mm)
        if c + 1 < NC:
            on("tensor", lambda te, c=c: _st2_pq(te, c + 1))

        def make_h2fe(tw, c=c):
            t = 4 * c + tw
            sl = t % 2

            def h2fm_evac(v, t=t, tw=tw, c=c, sl=sl):
                v.wait_ge(s["h2t"].h, tok[("h2t", t)])
                if c >= 2:
                    v.wait_ge(s["ffp"].h, tok[("ffp", 4 * (c - 2) + 3)])
                src = pT[:, 1024 * sl:1024 * (sl + 1)]
                dst = (h_fm[:, 4096 * (c % 2):4096 * (c % 2 + 1)]
                       .rearrange("p (k q) -> p k q", k=KD)[:, :, 128 * tw:128 * (tw + 1)])
                v.tensor_copy(dst, src.rearrange("p (k q) -> p k q", k=KD)
                              ).then_inc(s["h2fe"].h, 1)
                rec(("h2fe", t), s["h2fe"].inc(1))
            return h2fm_evac

        for j2 in range(2):
            t0 = 4 * c + 2 * j2
            pi = t0 // 2
            so2 = 2 * (pi % 2)

            def resid1_tile(v, t0, pi, so2, i):
                t = t0 + i
                bank = pwA if t % 2 == 0 else pwB
                v.wait_ge(s["wop"].h, tok[("wop", t)])
                xs = x_sb[:, 1024 * t:1024 * (t + 1)]
                v.tensor_tensor(xs, xs, bank[:], ALU.add).then_inc(s["r1"].h, 1)
                rec(("r1", t), s["r1"].inc(1))
                v.drain()
                if pi >= 2 and i == 0:
                    v.wait_ge(s["h2"].h, tok[("h2", t0 - 3)])
                v.bn_stats(stats[:, 12 * (so2 + i):12 * (so2 + i) + 6], xs[:, 0:512])
                v.bn_stats(stats[:, 12 * (so2 + i) + 6:12 * (so2 + i) + 12],
                           xs[:, 512:1024])
                v.drain()
                v.bn_aggr(mv[:, 2 * (so2 + i):2 * (so2 + i) + 2],
                          stats[:, 12 * (so2 + i):12 * (so2 + i) + 12])
                v.drain()
                v.tensor_scalar(veps[:, so2 + i:so2 + i + 1],
                                mv[:, 2 * (so2 + i) + 1:2 * (so2 + i) + 2],
                                EPS, None, ALU.add)
                ins = v.tensor_scalar(negm[:, so2 + i:so2 + i + 1],
                                      mv[:, 2 * (so2 + i):2 * (so2 + i) + 1],
                                      -1.0, None, ALU.mult)
                if i == 1:
                    ins.then_inc(s["ve2"].h, 1)
                    rec(("ve2p", pi), s["ve2"].inc(1))
                v.drain()
                sl = t % 2
                if t >= 2:
                    v.wait_ge(s["h2t"].h, tok[("h2t", t - 2)])
                v.tensor_scalar(h_tm[:, 1024 * sl:1024 * (sl + 1)], xs,
                                negm[:, so2 + i:so2 + i + 1],
                                None, ALU.add).then_inc(s["h2"].h, 1)
                rec(("h2", t), s["h2"].inc(1))

            def resid1_pair(v, t0=t0, pi=pi, so2=so2, j2=j2, c=c):
                if c == 3:
                    resid1_tile(v, t0, pi, so2, 0)
                    resid1_tile(v, t0, pi, so2, 1)
                    return
                for i in range(2):
                    t = t0 + i
                    bank = pwA if t % 2 == 0 else pwB
                    v.wait_ge(s["wop"].h, tok[("wop", t)])
                    xs = x_sb[:, 1024 * t:1024 * (t + 1)]
                    v.tensor_tensor(xs, xs, bank[:], ALU.add).then_inc(s["r1"].h, 1)
                    rec(("r1", t), s["r1"].inc(1))
                v.drain()
                if pi >= 2:
                    v.wait_ge(s["h2"].h, tok[("h2", t0 - 3)])
                for i in range(2):
                    t = t0 + i
                    xs = x_sb[:, 1024 * t:1024 * (t + 1)]
                    v.bn_stats(stats[:, 12 * (so2 + i):12 * (so2 + i) + 6],
                               xs[:, 0:512])
                    v.bn_stats(stats[:, 12 * (so2 + i) + 6:12 * (so2 + i) + 12],
                               xs[:, 512:1024])
                v.drain()
                for i in range(2):
                    v.bn_aggr(mv[:, 2 * (so2 + i):2 * (so2 + i) + 2],
                              stats[:, 12 * (so2 + i):12 * (so2 + i) + 12])
                v.drain()
                mvv = mv[:, 2 * so2:2 * so2 + 4].rearrange("p (a b) -> p a b", b=2)
                v.tensor_scalar(veps[:, so2:so2 + 2], mvv[:, :, 1:2].squeeze(2),
                                EPS, None, ALU.add)
                v.tensor_scalar(negm[:, so2:so2 + 2], mvv[:, :, 0:1].squeeze(2),
                                -1.0, None, ALU.mult).then_inc(s["ve2"].h, 1)
                rec(("ve2p", pi), s["ve2"].inc(1))
                v.drain()
                for i in range(2):
                    t = t0 + i
                    sl = t % 2
                    if t >= 2:
                        v.wait_ge(s["h2t"].h, tok[("h2t", t - 2)])
                    v.tensor_scalar(h_tm[:, 1024 * sl:1024 * (sl + 1)],
                                    x_sb[:, 1024 * t:1024 * (t + 1)],
                                    negm[:, so2 + i:so2 + i + 1],
                                    None, ALU.add).then_inc(s["h2"].h, 1)
                    rec(("h2", t), s["h2"].inc(1))
            on("vector", resid1_pair)
            if c == 3:
                if j2 == 0:
                    on("vector", make_h2fe(0))
                    on("vector", make_h2fe(1))

            def ln2_pair(sc, t0=t0, pi=pi, so2=so2):
                sc.wait_ge(s["ve2"].h, tok[("ve2p", pi)])
                sc.activation(lnv[:, so2:so2 + 2], veps[:, so2:so2 + 2], AF.Ln)
                sc.drain()
                sc.activation(rstd[:, so2:so2 + 2], lnv[:, so2:so2 + 2], AF.Exp,
                              scale=-0.5).then_inc(s["rsd2"].h, 1)
                rec(("rsd2", pi), s["rsd2"].inc(1))
            if j2 == 0 or c + 1 >= NC:
                on("scalar", ln2_pair)
                if c + 1 < NC:
                    on("scalar", lambda sc, c=c: _st2_pq_evac(sc, c + 1, range(0, 4)))
            else:
                on("scalar", lambda sc, c=c: _st2_pq_evac(sc, c + 1, range(4, 8)))
                on("scalar", ln2_pair)

        h2t_tws = range(TPC) if c < 3 else range(2)
        for tw in h2t_tws:
            t = 4 * c + tw
            sl = t % 2

            def h2_transpose(te, t=t, sl=sl):
                te.wait_ge(s["h2"].h, tok[("h2", t)])
                if t >= 2:
                    te.wait_ge(s["h2fe"].h, tok[("h2fe", t - 2)])
                else:
                    te.wait_ge(s["eae"].h, tok[("eae", 0, 1)])
                dst = pT[:, 1024 * sl:1024 * (sl + 1)]
                for k in range(KD):
                    ins = te.matmul(dst[:, 128 * k:128 * (k + 1)],
                                    h_tm[:, 1024 * sl + 128 * k:1024 * sl + 128 * (k + 1)],
                                    ident_sb[:], start=(k == 0), stop=(k == KD - 1),
                                    is_transpose=True)
                ins.then_inc(s["h2t"].h, 1)
                rec(("h2t", t), s["h2t"].inc(1))
            on("tensor", h2_transpose)

        if c < 3:
            for tw in range(TPC):
                on("vector", make_h2fe(tw))

        def ffn_tile(te, tw, c=c):
            if c == 0 and tw == 0:
                te.wait_ge(s["sgn"].h, tok[("sgn", KD - 1)])
            t = 4 * c + tw
            bank = pwA if tw % 2 == 0 else pwB
            te.wait_ge(s["h2fe"].h, tok[("h2fe", t)])
            if tw < 2:
                te.wait_ge(s["r1"].h, tok[("r1", t + 2)])
            else:
                te.wait_ge(s["r2"].h, tok[("r2", t - 2)])
            for j in range(2):
                for k in range(KD):
                    ins = te.matmul(bank[:, 512 * j:512 * (j + 1)],
                                    h_fm[:, 4096 * (c % 2) + 512 * k + 128 * tw:
                                         4096 * (c % 2) + 512 * k + 128 * (tw + 1)],
                                    wslot_b[:, 1024 * k + 512 * j:
                                            1024 * k + 512 * (j + 1)],
                                    start=(k == 0), stop=(k == KD - 1))
            ins.then_inc(s["ffp"].h, 1)
            rec(("ffp", t), s["ffp"].inc(1))

        def ffn_mm(te, c=c):
            for tw in range(TPC):
                ffn_tile(te, tw)
        if c < 3:
            on("tensor", ffn_mm)
        else:
            on("tensor", lambda te: (ffn_tile(te, 0), ffn_tile(te, 1)))
            for tw in (2, 3):
                t = 4 * c + tw
                sl = t % 2

                def h2_transpose_l(te, t=t, sl=sl):
                    te.wait_ge(s["h2"].h, tok[("h2", t)])
                    te.wait_ge(s["h2fe"].h, tok[("h2fe", t - 2)])
                    dst = pT[:, 1024 * sl:1024 * (sl + 1)]
                    for k in range(KD):
                        ins = te.matmul(dst[:, 128 * k:128 * (k + 1)],
                                        h_tm[:, 1024 * sl + 128 * k:
                                             1024 * sl + 128 * (k + 1)],
                                        ident_sb[:], start=(k == 0),
                                        stop=(k == KD - 1), is_transpose=True)
                    ins.then_inc(s["h2t"].h, 1)
                    rec(("h2t", t), s["h2t"].inc(1))
                on("tensor", h2_transpose_l)
            on("tensor", lambda te: (ffn_tile(te, 2), ffn_tile(te, 3)))

        for tw in range(TPC):
            t = 4 * c + tw

            def resid2(v, t=t, tw=tw):
                bank = pwA if tw % 2 == 0 else pwB
                v.wait_ge(s["ffp"].h, tok[("ffp", t)])
                v.wait_ge(s["rsd2"].h, tok[("rsd2", t // 2)])
                col = 2 * ((t // 2) % 2) + (t % 2)
                xs = x_sb[:, 1024 * t:1024 * (t + 1)]
                v.scalar_tensor_tensor(xs, bank[:], rstd[:, col:col + 1], xs,
                                       ALU.mult, ALU.add).then_inc(s["r2"].h, 1)
                rec(("r2", t), s["r2"].inc(1))
            on("vector", resid2)
            if c == 3 and tw < 2:
                on("vector", make_h2fe(tw + 2))

            def store(eng, t=t):
                eng.wait_ge(s["r2"].h, tok[("r2", t)])
                eng.dma_start(out=y_out[128 * t:128 * (t + 1), :],
                              in_=x_sb[:, 1024 * t:1024 * (t + 1)]
                              ).then_inc(s["st%d" % (t % 2)].h, 16)
                rec(("st", t), s["st%d" % (t % 2)].inc(16))
            on("sync", store)

    def final_wait(eng):
        eng.wait_ge(s["st0"].h, tok[("st", NT - 2)])
        eng.wait_ge(s["st1"].h, tok[("st", NT - 1)])
    on("sync", final_wait)

    es.close()
    return nc


_NC_CACHE = None


def kernel(L0_attn_x, L0_attn_W_emb, L0_attn_W_anchor, L0_attn_W_out,
           L0_attn_b_out, L0_ffn_weight):
    global _NC_CACHE
    x = np.ascontiguousarray(np.asarray(L0_attn_x, dtype=np.float32))
    W_emb = np.asarray(L0_attn_W_emb, dtype=np.float32)
    W_anchor = np.asarray(L0_attn_W_anchor, dtype=np.float32)
    W_out = np.asarray(L0_attn_W_out, dtype=np.float32)
    b_out = np.asarray(L0_attn_b_out, dtype=np.float32)
    W_ffn = np.asarray(L0_ffn_weight, dtype=np.float32)

    wqkT = np.ascontiguousarray(W_emb[:1024].T)
    wvT = np.ascontiguousarray(W_emb[1024:].T)
    wanchorT = np.ascontiguousarray(W_anchor.T)
    woutT = np.ascontiguousarray(W_out.T)
    wffnT = np.ascontiguousarray(W_ffn.T)
    boutr = np.ascontiguousarray(b_out.reshape(1, 1024))
    ident = np.eye(128, dtype=np.float32)

    if _NC_CACHE is None:
        _NC_CACHE = build_nc()
    nc = _NC_CACHE

    in_maps = []
    for c in range(8):
        b, half = c // 2, c % 2
        xs = np.ascontiguousarray(x[b, half * T:(half + 1) * T, :])
        in_maps.append({"x": xs, "wqkT": wqkT, "wvT": wvT, "woutT": woutT,
                        "wffnT": wffnT, "wanchorT": wanchorT, "bout": boutr,
                        "identin": ident})
    res = run_bass_kernel_spmd(nc, in_maps, core_ids=list(range(8)))
    out = np.empty((4, 4096, 1024), dtype=np.float32)
    for c in range(8):
        b, half = c // 2, c % 2
        out[b, half * T:(half + 1) * T, :] = res.results[c]["y"]
    return out
